# version 03
# baseline (speedup 1.0000x reference)
import sys
import time
import numpy as np
import ml_dtypes
import concourse.bass as bass
import concourse.mybir as mybir
from concourse.bass_utils import run_bass_kernel_spmd
from contextlib import ExitStack

F32 = mybir.dt.float32
BF16 = mybir.dt.bfloat16
I32 = mybir.dt.int32
AF = mybir.ActivationFunctionType
OP = mybir.AluOpType
AX = mybir.AxisListType

L = 2048
D = 2048
DC = 16
NLT = 16
K = 32
EPS = 1e-5
TWOPI = float(2 * np.pi)
TQ = 512
QN = L // TQ
INTER = 8192
OT = INTER // 128
TC = 512
LCN = L // TC


STAGE = 4
NOXN2 = NOTRP = NOEV = False


def build():
    nc = bass.Bass("TRN2", target_bir_lowering=False)
    x_d = nc.dram_tensor("x", [L, D], F32, kind="ExternalInput").ap()
    wmem_d = nc.dram_tensor("wmem", [128, DC * 288], F32, kind="ExternalInput").ap()
    wq_d = nc.dram_tensor("wq", [128, DC * 512], F32, kind="ExternalInput").ap()
    convk_d = nc.dram_tensor("convk", [3 * 128, 4], F32, kind="ExternalInput").ap()
    theta_d = nc.dram_tensor("theta", [2 * 128, 8], F32, kind="ExternalInput").ap()
    slopes_d = nc.dram_tensor("slopes", [K, 1], F32, kind="ExternalInput").ap()
    ss_d = nc.dram_tensor("ss", [K, 1], F32, kind="ExternalInput").ap()
    wr_d = nc.dram_tensor("wr", [2 * 128, 48], F32, kind="ExternalInput").ap()
    wi_d = nc.dram_tensor("wi", [2 * 128, 48], F32, kind="ExternalInput").ap()
    norm_d = nc.dram_tensor("norm", [2 * 128, 1], F32, kind="ExternalInput").ap()
    wout_d = nc.dram_tensor("wout", [768, D], F32, kind="ExternalInput").ap()
    x1_d = nc.dram_tensor("x1", [L, D], F32).ap()
    out_d = nc.dram_tensor("out", [L, D], F32, kind="ExternalOutput").ap()
    w1g_d = nc.dram_tensor("w1g", [OT, 128, DC, 128], mybir.dt.int8, kind="ExternalInput").ap()
    w1u_d = nc.dram_tensor("w1u", [OT, 128, DC, 128], mybir.dt.int8, kind="ExternalInput").ap()
    w2_d = nc.dram_tensor("w2", [INTER, D], mybir.dt.int8, kind="ExternalInput").ap()
    sg_d = nc.dram_tensor("sg", [128, OT], F32, kind="ExternalInput").ap()
    su_d = nc.dram_tensor("su", [128, OT], F32, kind="ExternalInput").ap()
    sd_d = nc.dram_tensor("sd", [1, D], F32, kind="ExternalInput").ap()

    q_scr = nc.dram_tensor("q_scr", [512, L], F32).ap()
    pw_scr = nc.dram_tensor("pw_scr", [K, L], F32).ap()
    id_scr = nc.dram_tensor("id_scr", [K, L], F32).ap()
    xn2_scr = nc.dram_tensor("xn2_scr", [L, D], mybir.dt.bfloat16).ap()

    identN = np.eye(128, dtype=ml_dtypes.bfloat16)
    ident_h = nc.inline_tensor(identN, name="identc")
    nident_h = nc.inline_tensor(np.asarray(-identN, dtype=ml_dtypes.bfloat16), name="nidentc")
    ar = np.arange(L, dtype=np.float32)
    dist_np = np.concatenate([np.tile((L - 1) - ar, (28, 1)), np.tile(ar, (4, 1))]).astype(np.float32)
    dist_h = nc.inline_tensor(dist_np, name="distc")
    maskk_np = (np.arange(128)[:, None] % 32 == np.arange(32)[None, :]).astype(np.float32)
    maskk_h = nc.inline_tensor(maskk_np, name="maskkc")

    with ExitStack() as ctx:
        def sb_(name, shape, dt_):
            return ctx.enter_context(nc.sbuf_tensor(name, shape, dt_))
        xT = sb_("xT", [128, DC * L], BF16)
        kval = sb_("kval", [128, 2 * L], F32)
        wob = sb_("wob", [128, 6 * 2048], BF16)
        wbd = sb_("wbd", [128, 4 * 1536], BF16)
        ax = sb_("ax", [128, 2 * D], F32)
        B = sb_("B", [128, 16384], F32)
        qst = B[:, 0:2048]
        zpads = [B[:, 2176:4228], B[:, 4352:6404], B[:, 6528:8580]]
        pwt = B[:, 8704:10752]
        dent = B[:, 10752:12800]
        dist_sb = B[:, 12800:14848]
        lwt = dist_sb
        wbdf = B[:, 0:1536]

        def b32(i):
            return B[:, 512 * i:512 * (i + 1)]
        T_psi = b32(0); T_kf = b32(1); T_r = b32(2); T_rc = b32(3)
        T_msk = b32(1)
        T_sin = b32(10); T_cos = b32(12)
        T_mre = [b32(6), b32(7)]
        T_mim = [b32(8), b32(9)]
        T_nre = [b32(4), b32(11)]
        T_nim = [b32(5), b32(13)]
        T_qre = [b32(14), b32(15)]
        T_qim = [b32(16), b32(17)]
        T_pwr = b32(18)
        T_idr = b32(19)
        P_bf = B[:, 512 * 20:512 * 24].bitcast(BF16)

        def pbf(i):
            return P_bf[:, 512 * i:512 * (i + 1)]
        T_ore = [B[:, 512 * 24 + 256 * i:512 * 24 + 256 * (i + 1)].bitcast(BF16) for i in range(4)]
        yswi = B[:, 512 * 26:512 * 29].bitcast(BF16)

        def yswi_t(j):
            return yswi[:, 512 * j:512 * (j + 1)]
        T_int = B[:, 512 * 29:512 * 30].bitcast(I32)
        T_carry = B[:, 512 * 30:512 * 30 + 40]
        sqout = B[:, 512 * 31:512 * 32]
        T_sg = b32(1)
        T_t1 = b32(0)

        wmem_sb = wbd[:, 0:DC * 288]
        wq_sb = wob[:, 0:DC * 512]
        convk_sb = sb_("convk_sb", [128, 12], F32)
        theta_sb = sb_("theta_sb", [128, 16], F32)
        theta_ss = sb_("theta_ss", [128, 16], F32)
        slop = sb_("slop", [K, 8], F32)
        ss_sb = sb_("ss_sb", [K, 1], F32)
        wr_sb = sb_("wr_sb", [128, 96], F32)
        wi_sb = sb_("wi_sb", [128, 96], F32)
        norm_sb = sb_("norm_sb", [128, 2], F32)
        maskk_sb = sb_("maskk_sb", [128, 32], F32)
        ident_sb = sb_("ident_sb", [128, 128], BF16)
        nident_sb = sb_("nident_sb", [128, 128], BF16)
        ones32 = sb_("ones32", [K, 1], F32)
        eps_sb = sb_("eps_sb", [128, 1], F32)
        spc = sb_("spc", [128, 1], F32)
        spc2 = sb_("spc2", [128, 1], F32)
        xnb = sb_("xnb", [128, 2 * D], BF16)
        sqpart = sb_("sqpart", [128, 4 * NLT], F32)
        sqparta = sb_("sqparta", [128, 4 * NLT], F32)
        sqpartb = sb_("sqpartb", [128, 4 * NLT], F32)
        stats = sb_("stats", [128, 18 * NLT], F32)

        def stat(j, lt):
            return stats[:, j * NLT + lt:j * NLT + lt + 1]
        msq = sb_("msq", [128, 2], F32)

        fuseF = B[:].bitcast(BF16)
        w1g_sb = wob[:, 0:4096]
        w1u_sb = wob[:, 4096:8192]
        w2f_sb = wob[:, 8192:12288]
        x1f = ax
        y2sb = kval
        sd_rep = wbd[:, 0:4096].bitcast(F32)
        tmp2F = xnb[:, 0:1024].bitcast(F32)
        fsqout = xnb[:, 1024:2048].bitcast(F32)
        fsig = xnb[:, 2048:4096]
        sg_sb = stats[:, 6 * NLT:6 * NLT + OT]
        su_sb = stats[:, 10 * NLT:10 * NLT + OT]
        sgu_sb = stats[:, 14 * NLT:14 * NLT + OT]
        psA = ctx.enter_context(nc.psum_tensor("psA", [128, 2048], F32))
        psB = ctx.enter_context(nc.psum_tensor("psB", [128, 2048], F32))
        trp = [psB[:, 0:64].bitcast(BF16), psB[:, 512:576].bitcast(BF16)]
        og = [psB[:, 0:512], psB[:, 512:1024], psB[:, 1024:1536], psB[:, 1536:2048]]
        yv_ps = psA[:, 0:512]
        yg_ps = psA[:, 512:1024]
        gate_ps = [psA[:, 0:512], psA[:, 1024:1536]]
        up_ps = [psA[:, 512:1024], psA[:, 1536:2048]]

        sems = {}
        for s in ["sxl", "ssq0", "svar0", "sstd0", "sxn0", "strp0", "stre0",
                  "sldc", "sgl", "stzq", "szq", "sqst", "scnv", "spw0", "spw",
                  "sden", "sbc", "swbd", "sqrp", "sphr", "ssin", "sphi", "sscn",
                  "sprod", "sprd2", "ssmc", "sog", "soev", "syv", "ssg", "syw",
                  "sao", "saoe", "ssqa", "sqba", "svara", "sstda",
                  "ssqb", "sqbb", "svarb", "sstdb",
                  "sxr", "sx1", "sx1st", "sx2n", "strp2", "stre2", "ssp", "sacc", "strn", "spir",
                  "fsld", "fx1", "fw1", "fw2", "ftg", "ffu", "fdn", "fy2", "fot", "fst",
                  "fsgm", "fsq", "fqb", "fvr", "fsd"]:
            sems[s] = ctx.enter_context(nc.semaphore(s))
        S = sems
        block = ctx.enter_context(nc.Block())


        def _ffn_sync(sync):
            sync.wait_ge(S["sx1st"], 16 * NLT)
            sync.wait_ge(S["stre2"], 16 * NLT)
            sync.dma_start(out=sg_sb, in_=sg_d).then_inc(S["fsld"], 16)
            sync.dma_start(out=su_sb, in_=su_d).then_inc(S["fsld"], 16)
            sync.dma_start(out=sd_rep,
                           in_=sd_d[0].unsqueeze(0).broadcast_to([128, D])).then_inc(S["fsld"], 16)
            for lt in range(2):
                sync.dma_start(out=x1f[:, (lt % 2) * D:(lt % 2 + 1) * D],
                               in_=x1_d[128 * lt:128 * (lt + 1), :]).then_inc(S["fx1"], 16)
            for lt in range(NLT):
                sync.wait_ge(S["fot"], lt + 1)
                sync.dma_start(out=out_d[128 * lt:128 * (lt + 1), :],
                               in_=y2sb[:, (lt % 2) * D:(lt % 2 + 1) * D]).then_inc(S["fst"], 16)
                if lt + 2 < NLT:
                    sync.dma_start(out=x1f[:, (lt % 2) * D:(lt % 2 + 1) * D],
                                   in_=x1_d[128 * (lt + 2):128 * (lt + 3), :]).then_inc(S["fx1"], 16)
            sync.wait_ge(S["fst"], 16 * NLT)

        def _ffn_gp(gp):
            gp.wait_ge(S["sao"], NLT)
            for lc in range(LCN):
                for ot in range(OT):
                    got = lc * OT + ot
                    if got >= 2:
                        gp.wait_ge(S["ftg"], 2 * (got - 1))
                    gp.dma_start(out=w1g_sb[:, (got % 2) * 2048:(got % 2 + 1) * 2048],
                                 in_=w1g_d[ot]).then_inc(S["fw1"], 16)
                    gp.dma_start(out=w1u_sb[:, (got % 2) * 2048:(got % 2 + 1) * 2048],
                                 in_=w1u_d[ot]).then_inc(S["fw1"], 16)
                for pz in range(2):
                    for i in range(OT):
                        g = (lc * 2 + pz) * OT + i
                        if g >= 2:
                            gp.wait_ge(S["fdn"], g - 1)
                        gp.dma_start(out=w2f_sb[:, (g % 2) * 2048:(g % 2 + 1) * 2048],
                                     in_=w2_d[128 * i:128 * (i + 1), :]).then_inc(S["fw2"], 16)

        def _ffn_t(t):
            t.wait_ge(S["strn"], 16 * DC)
            for lc in range(LCN):
                if lc >= 1:
                    t.wait_ge(S["fy2"], 2 * lc)
                for ot in range(OT):
                    got = lc * OT + ot
                    if got >= 2:
                        t.wait_ge(S["ffu"], got - 1)
                    t.wait_ge(S["fw1"], 32 * (got + 1))
                    bb = got % 2
                    for k in range(DC):
                        mm = t.matmul(gate_ps[bb],
                                      w1g_sb[:, bb * 2048 + 128 * k:bb * 2048 + 128 * (k + 1)],
                                      xT[:, k * L + lc * TC:k * L + lc * TC + TC],
                                      start=(k == 0), stop=(k == DC - 1))
                    mm.then_inc(S["ftg"], 1)
                    for k in range(DC):
                        mm = t.matmul(up_ps[bb],
                                      w1u_sb[:, bb * 2048 + 128 * k:bb * 2048 + 128 * (k + 1)],
                                      xT[:, k * L + lc * TC:k * L + lc * TC + TC],
                                      start=(k == 0), stop=(k == DC - 1))
                    mm.then_inc(S["ftg"], 1)
                t.wait_ge(S["ffu"], OT * (lc + 1))
                for pz in range(2):
                    if lc * 2 + pz >= 1:
                        t.wait_ge(S["fy2"], lc * 2 + pz)
                    for i in range(OT):
                        g = (lc * 2 + pz) * OT + i
                        t.wait_ge(S["fw2"], 16 * (g + 1))
                        for lt01 in range(2):
                            ltl = pz * 2 + lt01
                            dst = psB if lt01 == 0 else psA
                            for dc4 in range(4):
                                mm = t.matmul(dst[:, 512 * dc4:512 * (dc4 + 1)],
                                              fuseF[:, i * TC + ltl * 128:i * TC + ltl * 128 + 128],
                                              w2f_sb[:, (g % 2) * 2048 + 512 * dc4:(g % 2) * 2048 + 512 * (dc4 + 1)],
                                              start=(i == 0), stop=(i == OT - 1))
                        mm.then_inc(S["fdn"], 1)

        def _ffn_s(s):
            for lc in range(LCN):
                for ot in range(OT):
                    got = lc * OT + ot
                    s.wait_ge(S["ftg"], 2 * got + 1)
                    s.activation(fsig[:, (got % 2) * 512:(got % 2 + 1) * 512],
                                 gate_ps[got % 2], AF.Sigmoid,
                                 scale=sg_sb[:, ot:ot + 1]).then_inc(S["fsgm"], 1)
                for pz in range(2):
                    for lt01 in range(2):
                        lt = lc * 4 + pz * 2 + lt01
                        y2v = y2sb[:, lt01 * D:(lt01 + 1) * D]
                        s.wait_ge(S["fsq"], lt + 1)
                        for c4 in range(4):
                            act = s.activation(fsqout[:], y2v[:, 512 * c4:512 * (c4 + 1)], AF.Square,
                                               accum_out=sqpart[:, 4 * lt + c4:4 * lt + c4 + 1])
                        act.then_inc(S["fqb"], 1)
                        s.wait_ge(S["fvr"], lt + 1)
                        s.activation(stat(4, lt), stat(3, lt), AF.Sqrt, bias=eps_sb[:, 0:1])
                        s.activation(fsqout[:, 0:1], spc[:], AF.Sqrt).then_inc(S["fsd"], 1)

        def _ffn_v(v):
            v.wait_ge(S["fsld"], 32)
            v.tensor_mul(sgu_sb, sg_sb, su_sb)
            for lc in range(LCN):
                for ot in range(OT):
                    got = lc * OT + ot
                    bb = got % 2
                    v.wait_ge(S["ftg"], 2 * got + 2)
                    v.wait_ge(S["fsgm"], got + 1)
                    v.tensor_copy(spc[:], spc[:])
                    v.tensor_tensor(tmp2F, gate_ps[bb], fsig[:, bb * 512:(bb + 1) * 512], OP.mult)
                    v.tensor_tensor(tmp2F, tmp2F, up_ps[bb], OP.mult)
                    v.tensor_scalar(fuseF[:, ot * TC:(ot + 1) * TC], tmp2F,
                                    sgu_sb[:, ot:ot + 1], None, OP.mult).then_inc(S["ffu"], 1)
                for pz in range(2):
                    v.wait_ge(S["fdn"], 64 * (lc * 2 + pz + 1))
                    v.tensor_copy(spc[:], spc[:])
                    v.tensor_copy(spc[:], spc[:])
                    for lt01 in range(2):
                        lt = lc * 4 + pz * 2 + lt01
                        srcp = psB if lt01 == 0 else psA
                        y2v = y2sb[:, lt01 * D:(lt01 + 1) * D]
                        if lt >= 2:
                            v.wait_ge(S["fst"], 16 * (lt - 1))
                            v.wait_ge(S["fqb"], lt - 1)
                        mmv = v.tensor_tensor(y2v, srcp[:], sd_rep, OP.mult)
                        if lt01 == 1:
                            mmv.then_inc(S["fy2"], 1)
                        v.reduce_sum(stat(0, lt), y2v, axis=AX.X).then_inc(S["fsq"], 1)
                    for lt01 in range(2):
                        lt = lc * 4 + pz * 2 + lt01
                        y2v = y2sb[:, lt01 * D:(lt01 + 1) * D]
                        v.wait_ge(S["fqb"], lt + 1)
                        v.reduce_sum(stat(1, lt), sqpart[:, 4 * lt:4 * lt + 4], axis=AX.X)
                        v.tensor_scalar(stat(2, lt), stat(0, lt), 1.0 / D, None, OP.mult)
                        v.tensor_scalar(stat(3, lt), stat(1, lt), 1.0 / D, None, OP.mult)
                        v.tensor_mul(msq[:, 0:1], stat(2, lt), stat(2, lt))
                        v.tensor_copy(spc[:], spc[:])
                        v.tensor_sub(stat(3, lt), stat(3, lt), msq[:, 0:1]).then_inc(S["fvr"], 1)
                        v.wait_ge(S["fsd"], lt + 1)
                        v.tensor_copy(spc[:], spc[:])
                        v.reciprocal(stat(5, lt), stat(4, lt))
                        v.tensor_copy(spc[:], spc[:])
                        v.tensor_scalar(y2v, y2v, stat(2, lt), stat(5, lt),
                                        OP.subtract, OP.mult)
                        v.wait_ge(S["fx1"], 16 * (lt + 1))
                        v.tensor_add(y2v, y2v, x1f[:, (lt % 2) * D:(lt % 2 + 1) * D]).then_inc(S["fot"], 1)

        NCONST = 16

        def jidx(tc, m, half):
            return (tc * 8 + m) * 2 + half

        @block.sync
        def _(sync):
            sync.dma_start(out=convk_sb[:, 0:4], in_=convk_d[0:128, :]).then_inc(S["sldc"], 16)
            sync.dma_start(out=convk_sb[:, 4:8], in_=convk_d[128:256, :]).then_inc(S["sldc"], 16)
            sync.dma_start(out=convk_sb[:, 8:12], in_=convk_d[256:384, :]).then_inc(S["sldc"], 16)
            sync.dma_start(out=theta_sb[:, 0:8], in_=theta_d[0:128, :]).then_inc(S["sldc"], 16)
            sync.dma_start(out=theta_sb[:, 8:16], in_=theta_d[128:256, :]).then_inc(S["sldc"], 16)
            sync.dma_start(out=slop[:, 0:1], in_=slopes_d).then_inc(S["sldc"], 16)
            sync.dma_start(out=ss_sb[:], in_=ss_d).then_inc(S["sldc"], 16)
            sync.dma_start(out=wr_sb[:, 0:48], in_=wr_d[0:128, :]).then_inc(S["sldc"], 16)
            sync.dma_start(out=wr_sb[:, 48:96], in_=wr_d[128:256, :]).then_inc(S["sldc"], 16)
            sync.dma_start(out=wi_sb[:, 0:48], in_=wi_d[0:128, :]).then_inc(S["sldc"], 16)
            sync.dma_start(out=wi_sb[:, 48:96], in_=wi_d[128:256, :]).then_inc(S["sldc"], 16)
            sync.dma_start(out=norm_sb[:, 0:1], in_=norm_d[0:128, :]).then_inc(S["sldc"], 16)
            sync.dma_start(out=norm_sb[:, 1:2], in_=norm_d[128:256, :]).then_inc(S["sldc"], 16)
            sync.dma_start(out=ident_sb[:], in_=ident_h.ap()).then_inc(S["sldc"], 16)
            sync.dma_start(out=nident_sb[:], in_=nident_h.ap()).then_inc(S["sldc"], 16)
            sync.dma_start(out=maskk_sb[:], in_=maskk_h.ap()).then_inc(S["sldc"], 16)
            sync.dma_start(out=dist_sb[0:32, :], in_=dist_h.ap()).then_inc(S["sldc"], 16)
            for lt in range(NLT):
                if lt >= 2:
                    sync.wait_ge(S["sxn0"], lt - 1)
                    sync.wait_ge(S["ssq0"], lt - 1)
                sync.dma_start(out=ax[:, (lt % 2) * D:(lt % 2 + 1) * D],
                               in_=x_d[128 * lt:128 * (lt + 1), :]).then_inc(S["sxl"], 16)
            if STAGE == 0:
                sync.wait_ge(S["stre0"], 256)
                sync.dma_start(out=x1_d[0:128, :], in_=ax[:, 0:D]).then_inc(S["sx1st"], 16)
                sync.wait_ge(S["sx1st"], 16)
                return
            for qt in range(4):
                sync.wait_ge(S["szq"], 4 + qt)
                sync.dma_start(out=q_scr[128 * qt:128 * (qt + 1), :], in_=qst).then_inc(S["sqst"], 16)
            sync.wait_ge(S["spw"], 1)
            sync.dma_start(out=pw_scr, in_=pwt[0:32, :]).then_inc(S["sbc"], 16)
            sync.wait_ge(S["sden"], 1)
            sync.dma_start(out=id_scr, in_=dent[0:32, :]).then_inc(S["sbc"], 16)
            if STAGE == 1:
                sync.wait_ge(S["sbc"], 32)
                sync.wait_ge(S["swbd"], 4)
                sync.wait_ge(S["sgl"], 128)
                sync.dma_start(out=x1_d[0:128, :], in_=ax[:, 0:D]).then_inc(S["sx1st"], 16)
                sync.wait_ge(S["sx1st"], 16)
                return
            for tc in range(QN):
                t0 = TQ * tc
                sync.wait_ge(S["sbc"], 32)
                if tc >= 1:
                    sync.wait_ge(S["sphi"], 16 * tc)
                    sync.wait_ge(S["soev"], tc)
                sync.dma_start(out=T_pwr,
                               in_=pw_scr[:, t0:t0 + TQ].unsqueeze(0).broadcast_to([4, K, TQ])).then_inc(S["sqrp"], 16)
                sync.dma_start(out=T_idr,
                               in_=id_scr[:, t0:t0 + TQ].unsqueeze(0).broadcast_to([4, K, TQ])).then_inc(S["sqrp"], 16)
                for m in range(8):
                    for half in range(2):
                        j = jidx(tc, m, half)
                        if j >= 2:
                            sync.wait_ge(S["sprod"], j - 1)
                        base = m * 32 + 16 * half
                        for ri in range(2):
                            dst = (T_qre if ri == 0 else T_qim)[half]
                            srows = q_scr[256 * ri + base:256 * ri + base + 16, t0:t0 + TQ]
                            src = srows.rearrange("(h q) t -> h q t", q=4).unsqueeze(2)
                            src = src.broadcast_to([4, 4, 8, TQ])
                            sync.dma_start(out=dst, in_=src).then_inc(S["sqrp"], 16)
                if STAGE >= 4:
                    for l4 in range(4):
                        lt = tc * 4 + l4
                        sync.wait_ge(S["sacc"], 16 * (lt + 1))
                        sync.dma_start(out=x1_d[128 * lt:128 * (lt + 1), :],
                                       in_=ax[:, (lt % 2) * D:(lt % 2 + 1) * D]).then_inc(S["sx1st"], 16)
                        sync.wait_ge(S["sx2n"], lt + 1)
                        sync.dma_start(out=xn2_scr[128 * lt:128 * (lt + 1), :],
                                       in_=xnb[:, (lt % 2) * D:(lt % 2 + 1) * D]).then_inc(S["stre2"], 16)
            if STAGE >= 4:
                sync.wait_ge(S["sx1st"], 16 * NLT)
                sync.wait_ge(S["stre2"], 16 * NLT)
                for k in range(DC):
                    sync.dma_start_transpose(xT[:, L * k:L * (k + 1)],
                                             xn2_scr[:, 128 * k:128 * (k + 1)]).then_inc(S["strn"], 16)
                _ffn_sync(sync)
            elif STAGE == 3:
                sync.wait_ge(S["saoe"], NLT)
                sync.dma_start(out=x1_d[0:128, :], in_=ax[:, 0:D]).then_inc(S["sx1st"], 16)
                sync.wait_ge(S["sx1st"], 16)
            else:
                sync.wait_ge(S["syw"], 6 * QN)
                sync.dma_start(out=x1_d[0:128, :], in_=ax[:, 0:D]).then_inc(S["sx1st"], 16)
                sync.wait_ge(S["sx1st"], 16)

        @block.gpsimd
        def _(gp):
            gp.dma_start(out=wmem_sb[:], in_=wmem_d).then_inc(S["sgl"], 16)
            gp.dma_start(out=wq_sb[:], in_=wq_d).then_inc(S["sgl"], 16)
            if STAGE == 0:
                return
            gp.wait_ge(S["sldc"], 16 * (NCONST + 1))
            gp.wait_ge(S["sqst"], 16 * 4)
            for half in range(2):
                for ri, wsrc in [(0, wr_sb), (1, wi_sb)]:
                    for k in range(K):
                        gp.tensor_scalar(wbdf[:, 24 * k:24 * (k + 1)],
                                         wsrc[:, 48 * half:48 * half + 24],
                                         maskk_sb[:, k:k + 1], None, OP.mult)
                        gp.tensor_scalar(wbdf[:, 768 + 24 * k:768 + 24 * (k + 1)],
                                         wsrc[:, 48 * half + 24:48 * half + 48],
                                         maskk_sb[:, k:k + 1], None, OP.mult)
                    idx = half * 2 + ri
                    gp.tensor_scalar(wbd[:, 1536 * idx:1536 * (idx + 1)], wbdf[:],
                                     norm_sb[:, half:half + 1], None, OP.mult).then_inc(S["swbd"], 1)
            for j in range(6):
                gp.dma_start(out=wob[:, 2048 * j:2048 * (j + 1)],
                             in_=wout_d[128 * j:128 * (j + 1), :]).then_inc(S["sgl"], 16)
            gp.wait_ge(S["szq"], 3)
            for ct in range(3):
                dst = [kval[:, 0:L], kval[:, L:2 * L], pwt][ct]
                zp = zpads[ct]
                gp.tensor_scalar(dst, zp[:, 1:1 + L], convk_sb[:, 4 * ct:4 * ct + 1], None, OP.mult)
                for w in range(1, 4):
                    gp.tensor_scalar(dent, zp[:, w + 1:w + 1 + L],
                                     convk_sb[:, 4 * ct + w:4 * ct + w + 1], None, OP.mult)
                    gp.tensor_tensor(dst, dst, dent, OP.add)
                gp.tensor_copy(spc2[:], spc2[:]).then_inc(S["scnv"], 1)
            if STAGE < 2:
                return
            for tc in range(QN):
                for m in range(8):
                    for half in range(2):
                        j = jidx(tc, m, half)
                        jb = j % 2
                        gp.wait_ge(S["sscn"], j + 1)
                        c = 4 * m + 2 * half
                        if tc < QN - 1:
                            gp.tensor_copy(T_carry[:, c:c + 1], T_nre[jb][:, TQ - 1:TQ])
                            gp.tensor_copy(T_carry[:, c + 1:c + 2], T_nim[jb][:, TQ - 1:TQ])
                        gp.tensor_copy(spc2[:], spc2[:])
                        gp.tensor_copy(spc2[:], spc2[:]).then_inc(S["sprd2"], 1)
                if STAGE >= 4:
                    for l4 in range(4):
                        lt = tc * 4 + l4
                        gp.wait_ge(S["sxr"], lt + 1)
                        gp.dma_start(out=ax[:, (lt % 2) * D:(lt % 2 + 1) * D],
                                     in_=x_d[128 * lt:128 * (lt + 1), :],
                                     accum_op=OP.add).then_inc(S["sacc"], 16)

            _ffn_gp(gp)

        @block.tensor
        def _(t):
            for lt in range(NLT):
                t.wait_ge(S["sxn0"], lt + 1)
                for k in range(DC):
                    g = lt * DC + k
                    if g >= 2:
                        t.wait_ge(S["stre0"], g - 1)
                    t.transpose(trp[g % 2][:, 0:128],
                                xnb[:, (lt % 2) * D + 128 * k:(lt % 2) * D + 128 * (k + 1)],
                                ident_sb[:]).then_inc(S["strp0"], 1)
            if STAGE == 0:
                return
            t.wait_ge(S["stre0"], 256)
            t.wait_ge(S["sgl"], 32)
            for ct in range(3):
                wdt = 128 if ct < 2 else 32
                if ct >= 1:
                    t.wait_ge(S["szq"], ct)
                for tch in range(4):
                    for k in range(DC):
                        mm = t.matmul(psA[0:wdt, 512 * tch:512 * (tch + 1)],
                                      wmem_sb[:, 288 * k + 128 * ct:288 * k + 128 * ct + wdt],
                                      xT[:, L * k + 512 * tch:L * k + 512 * (tch + 1)],
                                      start=(k == 0), stop=(k == DC - 1))
                mm.then_inc(S["stzq"], 1)
            for qt in range(4):
                t.wait_ge(S["szq"], 3 + qt)
                for tch in range(4):
                    for k in range(DC):
                        mm = t.matmul(psA[:, 512 * tch:512 * (tch + 1)],
                                      wq_sb[:, 512 * k + 128 * qt:512 * k + 128 * (qt + 1)],
                                      xT[:, L * k + 512 * tch:L * k + 512 * (tch + 1)],
                                      start=(k == 0), stop=(k == DC - 1))
                mm.then_inc(S["stzq"], 1)
            if STAGE < 2:
                return
            for tc in range(QN):
                if tc >= 1:
                    t.wait_ge(S["soev"], tc)
                    if STAGE >= 4 and not NOXN2 and not NOTRP and not NOEV:
                        t.wait_ge(S["stre2"], 64 * tc)
                for m in range(8):
                    for half in range(2):
                        j = jidx(tc, m, half)
                        jb = j % 2
                        t.wait_ge(S["sprod"], j + 1)
                        st = (m == 0)
                        sp = (m == 7)
                        t.matmul(og[half], ident_sb[:], pbf(4 * jb + 0), start=st, stop=False)
                        t.matmul(og[half], ident_sb[:], pbf(4 * jb + 1), start=False, stop=sp)
                        t.matmul(og[2 + half], ident_sb[:], pbf(4 * jb + 2), start=st, stop=False)
                        mm = t.matmul(og[2 + half], ident_sb[:], pbf(4 * jb + 3), start=False, stop=sp)
                        mm.then_inc(S["ssmc"], 1)
                t.wait_ge(S["swbd"], 4)
                t.wait_ge(S["soev"], tc + 1)
                if tc >= 1:
                    t.wait_ge(S["saoe"], 4 * tc)
                for jy in range(6):
                    gy = tc * 6 + jy
                    if gy >= 1:
                        t.wait_ge(S["syw"], gy)
                    first = True
                    for half in range(2):
                        for ri in range(2):
                            idx = half * 2 + ri
                            last = (half == 1 and ri == 1)
                            t.matmul(yv_ps, wbd[:, 1536 * idx + 128 * jy:1536 * idx + 128 * (jy + 1)],
                                     T_ore[ri * 2 + half], start=first, stop=last)
                            mm = t.matmul(yg_ps, wbd[:, 1536 * idx + 768 + 128 * jy:1536 * idx + 768 + 128 * (jy + 1)],
                                          T_ore[ri * 2 + half], start=first, stop=last)
                            first = False
                    mm.then_inc(S["syv"], 1)
                if STAGE < 3:
                    continue
                t.wait_ge(S["syw"], 6 * (tc + 1))
                t.wait_ge(S["sgl"], 16 * 8)
                for l4 in range(4):
                    lt = tc * 4 + l4
                    if lt >= 1:
                        t.wait_ge(S["saoe"], lt)
                    for jy in range(6):
                        for dcc in range(4):
                            mm = t.matmul(psA[:, 512 * dcc:512 * (dcc + 1)],
                                          yswi_t(jy)[:, 128 * l4:128 * (l4 + 1)],
                                          wob[:, 2048 * jy + 512 * dcc:2048 * jy + 512 * (dcc + 1)],
                                          start=(jy == 0), stop=(jy == 5))
                    mm.then_inc(S["sao"], 1)

                continue

            _ffn_t(t)

        @block.scalar
        def _(s):
            for lt in range(NLT):
                s.wait_ge(S["sxl"], 16 * (lt + 1))
                xv = ax[:, (lt % 2) * D:(lt % 2 + 1) * D]
                for c4 in range(4):
                    act = s.activation(sqout[:], xv[:, 512 * c4:512 * (c4 + 1)], AF.Square,
                                       accum_out=sqpart[:, 4 * lt + c4:4 * lt + c4 + 1])
                act.then_inc(S["ssq0"], 1)
                s.wait_ge(S["svar0"], lt + 1)
                s.activation(stat(4, lt), stat(3, lt), AF.Sqrt,
                             bias=eps_sb[:, 0:1])
                s.activation(sqout[:, 0:1], spc[:], AF.Sqrt).then_inc(S["sstd0"], 1)
                for k in range(DC):
                    g = lt * DC + k
                    s.wait_ge(S["strp0"], g + 1)
                    s.copy(xT[:, L * k + 128 * lt:L * k + 128 * (lt + 1)],
                           trp[g % 2][:, 0:128]).then_inc(S["stre0"], 1)
            if STAGE == 0:
                return
            s.wait_ge(S["ssp"], 2)
            s.activation(slop[:, 4:5], slop[:, 2:3], AF.Exp, scale=-1.0)
            s.activation(sqout[0:K, 0:1], spc[0:K, :], AF.Exp)
            s.activation(slop[:, 5:6], slop[:, 4:5], AF.Ln, bias=ones32[:, 0:1])
            s.activation(sqout[0:K, 0:1], spc[0:K, :], AF.Exp)
            s.activation(slop[:, 6:7], slop[:, 5:6], AF.Identity, bias=slop[:, 3:4])
            s.activation(sqout[0:K, 0:1], spc[0:K, :], AF.Exp)
            s.activation(sqout[0:K, 1:2], spc[0:K, :], AF.Exp)
            s.mul(slop[:, 7:8], slop[:, 6:7], -1.0)
            s.activation(sqout[0:K, 0:1], spc[0:K, :], AF.Exp)
            s.activation(sqout[0:K, 0:1], spc[0:K, :], AF.Exp).then_inc(S["ssp"], 1)
            s.wait_ge(S["spw0"], 1)
            s.activation(pwt[0:K, :], pwt[0:K, :], AF.Exp).then_inc(S["spw"], 1)
            if STAGE < 2:
                return
            for tc in range(QN):
                for m in range(8):
                    for half in range(2):
                        j = jidx(tc, m, half)
                        s.wait_ge(S["sphr"], j + 1)
                        s.activation(T_sin, T_r, AF.Sin, scale=TWOPI)
                        s.activation(T_cos, T_rc, AF.Sin, scale=TWOPI).then_inc(S["ssin"], 1)
                for jy in range(6):
                    gy = tc * 6 + jy
                    s.wait_ge(S["syv"], gy + 1)
                    s.activation(sqout[:, 0:1], spc[:], AF.Sigmoid)
                    s.activation(T_sg, yg_ps, AF.Sigmoid).then_inc(S["ssg"], 1)
                if STAGE < 4:
                    continue
                for l4 in range(4):
                    lt = tc * 4 + l4
                    axv = ax[:, (lt % 2) * D:(lt % 2 + 1) * D]
                    s.wait_ge(S["ssqa"], lt + 1)
                    for c4 in range(4):
                        act = s.activation(sqout[:], axv[:, 512 * c4:512 * (c4 + 1)], AF.Square,
                                           accum_out=sqparta[:, 4 * lt + c4:4 * lt + c4 + 1])
                    act.then_inc(S["sqba"], 1)
                    s.wait_ge(S["svara"], lt + 1)
                    s.activation(stat(10, lt), stat(9, lt), AF.Sqrt,
                                 bias=eps_sb[:, 0:1])
                    s.activation(sqout[:, 0:1], spc[:], AF.Sqrt).then_inc(S["sstda"], 1)
                    s.wait_ge(S["ssqb"], lt + 1)
                    for c4 in range(4):
                        act = s.activation(sqout[:], axv[:, 512 * c4:512 * (c4 + 1)], AF.Square,
                                           accum_out=sqpartb[:, 4 * lt + c4:4 * lt + c4 + 1])
                    act.then_inc(S["sqbb"], 1)
                    s.wait_ge(S["svarb"], lt + 1)
                    s.activation(stat(16, lt), stat(15, lt), AF.Sqrt,
                                 bias=eps_sb[:, 0:1])
                    s.activation(sqout[:, 0:1], spc[:], AF.Sqrt).then_inc(S["sstdb"], 1)


            _ffn_s(s)

        @block.vector
        def _(v):
            v.memset(eps_sb[:], EPS)
            v.memset(spc[:], 1.0)
            v.memset(ones32[:], 1.0)
            v.memset(spc2[:], 1.0)
            for zp in zpads:
                v.memset(zp[:, 0:4], 0.0)
            v.wait_ge(S["sldc"], 16 * (NCONST + 1))
            v.tensor_scalar(theta_ss[:], theta_sb[:], float(1.0 / TWOPI), None, OP.mult)
            v.wait_ge(S["sldc"], 16 * (NCONST + 1))
            v.tensor_scalar(slop[:, 1:2], slop[:, 0:1], -1.0, None, OP.mult)
            v.tensor_scalar(slop[:, 3:4], slop[:, 0:1], 0.0, None, OP.max)
            v.tensor_tensor(slop[:, 2:3], slop[:, 0:1], slop[:, 1:2], OP.max).then_inc(S["ssp"], 2)
            for lt in range(NLT):
                xv = ax[:, (lt % 2) * D:(lt % 2 + 1) * D]
                v.wait_ge(S["sxl"], 16 * (lt + 1))
                v.reduce_sum(stat(0, lt), xv, axis=AX.X)
                v.wait_ge(S["ssq0"], lt + 1)
                v.reduce_sum(stat(1, lt), sqpart[:, 4 * lt:4 * lt + 4], axis=AX.X)
                v.tensor_scalar(stat(2, lt), stat(0, lt), 1.0 / D, None, OP.mult)
                v.tensor_scalar(stat(3, lt), stat(1, lt), 1.0 / D, None, OP.mult)
                v.tensor_mul(msq[:, 0:1], stat(2, lt), stat(2, lt))
                v.tensor_copy(spc[:], spc[:])
                v.tensor_sub(stat(3, lt), stat(3, lt), msq[:, 0:1]).then_inc(S["svar0"], 1)
                v.wait_ge(S["sstd0"], lt + 1)
                v.tensor_copy(spc[:], spc[:])
                v.reciprocal(stat(5, lt), stat(4, lt))
                if lt >= 2:
                    v.wait_ge(S["strp0"], 16 * (lt - 1))
                v.tensor_copy(spc[:], spc[:])
                v.tensor_scalar(xnb[:, (lt % 2) * D:(lt % 2 + 1) * D], xv, stat(2, lt), stat(5, lt),
                                OP.subtract, OP.mult).then_inc(S["sxn0"], 1)
            if STAGE == 0:
                return
            for ct in range(3):
                v.wait_ge(S["stzq"], ct + 1)
                v.tensor_copy(spc[:], spc[:])
                wdt = 128 if ct < 2 else 32
                v.tensor_copy(zpads[ct][0:wdt, 4:4 + L], psA[0:wdt, :]).then_inc(S["szq"], 1)
            for qt in range(4):
                v.wait_ge(S["stzq"], 4 + qt)
                v.tensor_copy(spc[:], spc[:])
                if qt >= 1:
                    v.wait_ge(S["sqst"], 16 * qt)
                v.tensor_copy(qst, psA[:]).then_inc(S["szq"], 1)
            v.wait_ge(S["scnv"], 3)
            v.tensor_scalar(pwt[0:K, :], pwt[0:K, :], ss_sb[:, 0:1], None, OP.mult)
            v.tensor_scalar(pwt[0:K, :], pwt[0:K, :], 20.0, None, OP.min)
            v.tensor_scalar(pwt[0:K, :], pwt[0:K, :], -20.0, None, OP.max)
            v.wait_ge(S["ssp"], 3)
            v.tensor_copy(spc[:], spc[:])
            v.tensor_copy(spc[:], spc[:])
            v.wait_ge(S["sldc"], 16 * (NCONST + 1))
            v.tensor_scalar(lwt[0:K, :], dist_sb[0:K, :], slop[:, 7:8], None, OP.mult)
            v.tensor_add(pwt[0:K, :], pwt[0:K, :], lwt[0:K, :]).then_inc(S["spw0"], 1)
            v.wait_ge(S["spw"], 1)
            v.tensor_tensor_scan(dent[0:K, :], pwt[0:K, :], pwt[0:K, :], 0.0, OP.add, OP.bypass)
            v.tensor_scalar(dent[0:K, :], dent[0:K, :], 1e-4, None, OP.max)
            v.reciprocal(dent[0:K, :], dent[0:K, :]).then_inc(S["sden"], 1)
            if STAGE < 2:
                return
            for tc in range(QN):
                for m in range(8):
                    for half in range(2):
                        j = jidx(tc, m, half)
                        jb = j % 2
                        if m == 0 and half == 0:
                            if tc == 0:
                                v.wait_ge(S["swbd"], 4)
                            v.wait_ge(S["sqrp"], 544 * tc + 32)
                        v.tensor_scalar(T_psi, kval[:, half * L + TQ * tc:half * L + TQ * tc + TQ],
                                        theta_ss[:, 8 * half + m:8 * half + m + 1], None, OP.mult)
                        v.tensor_copy(T_int, T_psi)
                        v.tensor_copy(T_kf, T_int)
                        v.tensor_tensor(T_r, T_psi, T_kf, OP.subtract)
                        v.tensor_scalar(T_rc, T_r, 0.25, None, OP.add)
                        v.tensor_copy(T_int, T_rc)
                        v.tensor_copy(T_kf, T_int)
                        v.tensor_tensor(T_rc, T_rc, T_kf, OP.subtract).then_inc(S["sphr"], 1)
                        v.wait_ge(S["ssin"], j + 1)
                        v.tensor_tensor(T_mre[jb], T_cos, T_pwr, OP.mult)
                        v.tensor_tensor(T_mim[jb], T_sin, T_pwr, OP.mult).then_inc(S["sphi"], 1)
                        if j >= 2:
                            v.wait_ge(S["sprd2"], j - 1)
                        c = 4 * m + 2 * half
                        ini_re = 0.0 if tc == 0 else T_carry[:, c:c + 1]
                        ini_im = 0.0 if tc == 0 else T_carry[:, c + 1:c + 2]
                        v.tensor_tensor_scan(T_nre[jb], T_mre[jb], T_mre[jb], ini_re, OP.add, OP.bypass)
                        v.tensor_tensor_scan(T_nim[jb], T_mim[jb], T_mim[jb], ini_im, OP.add, OP.bypass).then_inc(S["sscn"], 1)
                        v.wait_ge(S["sqrp"], 544 * tc + 32 + 32 * (2 * m + half + 1))
                        if j >= 2:
                            v.wait_ge(S["ssmc"], j - 1)
                        v.tensor_tensor(pbf(4 * jb + 0), T_nre[jb], T_qre[half], OP.mult)
                        v.tensor_tensor(pbf(4 * jb + 1), T_nim[jb], T_qim[half], OP.mult)
                        v.tensor_tensor(pbf(4 * jb + 2), T_nim[jb], T_qre[half], OP.mult)
                        v.tensor_tensor(pbf(4 * jb + 3), T_nre[jb], T_qim[half], OP.mult)
                        v.tensor_scalar(pbf(4 * jb + 3), pbf(4 * jb + 3), -1.0, None, OP.mult)
                        v.tensor_copy(spc[:], spc[:]).then_inc(S["sprod"], 1)
                v.wait_ge(S["ssmc"], 16 * (tc + 1))
                v.tensor_copy(spc[:], spc[:])
                v.tensor_copy(spc[:], spc[:])
                for gi in range(4):
                    v.tensor_tensor(T_ore[gi], og[gi], T_idr, OP.mult)
                v.tensor_copy(spc[:], spc[:])
                v.tensor_copy(spc[:], spc[:])
                v.tensor_copy(spc[:], spc[:])
                v.tensor_copy(spc[:], spc[:]).then_inc(S["soev"], 1)
                for jy in range(6):
                    gy = tc * 6 + jy
                    v.wait_ge(S["ssg"], gy + 1)
                    v.tensor_copy(spc[:], spc[:])
                    v.tensor_tensor(T_t1, yg_ps, T_sg, OP.mult)
                    v.tensor_tensor(yswi_t(jy), T_t1, yv_ps, OP.mult).then_inc(S["syw"], 1)
                if STAGE < 3:
                    continue
                for l4 in range(4):
                    lt = tc * 4 + l4
                    axv = ax[:, (lt % 2) * D:(lt % 2 + 1) * D]
                    v.wait_ge(S["sao"], lt + 1)
                    v.tensor_copy(spc[:], spc[:])
                    if lt >= 2 and STAGE >= 4:
                        v.wait_ge(S["sx1st"], 16 * (lt - 1))
                        v.wait_ge(S["sqbb"], lt - 1)
                    v.tensor_copy(axv, psA[:]).then_inc(S["saoe"], 1)
                    if STAGE < 4:
                        continue
                    v.reduce_sum(stat(6, lt), axv, axis=AX.X).then_inc(S["ssqa"], 1)
                    v.wait_ge(S["sqba"], lt + 1)
                    v.reduce_sum(stat(7, lt), sqparta[:, 4 * lt:4 * lt + 4], axis=AX.X)
                    v.tensor_scalar(stat(8, lt), stat(6, lt), 1.0 / D, None, OP.mult)
                    v.tensor_scalar(stat(9, lt), stat(7, lt), 1.0 / D, None, OP.mult)
                    v.tensor_mul(msq[:, 0:1], stat(8, lt), stat(8, lt))
                    v.tensor_copy(spc[:], spc[:])
                    v.tensor_sub(stat(9, lt), stat(9, lt), msq[:, 0:1]).then_inc(S["svara"], 1)
                    v.wait_ge(S["sstda"], lt + 1)
                    v.tensor_copy(spc[:], spc[:])
                    v.reciprocal(stat(11, lt), stat(10, lt))
                    v.tensor_copy(spc[:], spc[:])
                    v.tensor_scalar(axv, axv, stat(8, lt), stat(11, lt),
                                    OP.subtract, OP.mult).then_inc(S["sxr"], 1)
                    v.wait_ge(S["sacc"], 16 * (lt + 1))
                    v.reduce_sum(stat(12, lt), axv, axis=AX.X).then_inc(S["ssqb"], 1)
                    v.wait_ge(S["sqbb"], lt + 1)
                    v.reduce_sum(stat(13, lt), sqpartb[:, 4 * lt:4 * lt + 4], axis=AX.X)
                    v.tensor_scalar(stat(14, lt), stat(12, lt), 1.0 / D, None, OP.mult)
                    v.tensor_scalar(stat(15, lt), stat(13, lt), 1.0 / D, None, OP.mult)
                    v.tensor_mul(msq[:, 1:2], stat(14, lt), stat(14, lt))
                    v.tensor_copy(spc[:], spc[:])
                    v.tensor_sub(stat(15, lt), stat(15, lt), msq[:, 1:2]).then_inc(S["svarb"], 1)
                    v.wait_ge(S["sstdb"], lt + 1)
                    v.tensor_copy(spc[:], spc[:])
                    v.reciprocal(stat(17, lt), stat(16, lt))
                    if NOXN2:
                        continue
                    if lt >= 2:
                        v.wait_ge(S["stre2"], 16 * (lt - 1))
                    v.tensor_copy(spc[:], spc[:])
                    v.tensor_scalar(xnb[:, (lt % 2) * D:(lt % 2 + 1) * D], axv,
                                    stat(14, lt), stat(17, lt), OP.subtract, OP.mult).then_inc(S["sx2n"], 1)

            _ffn_v(v)

    return nc


def host_layouts(ins):
    W_mem = np.asarray(ins["L0_attn_W_mem"])
    conv_k = np.asarray(ins["L0_attn_conv_k"])
    W_q = np.asarray(ins["L0_attn_W_q"])
    theta = np.asarray(ins["L0_attn_theta"]).reshape(32, 8, 8)
    W_re = np.asarray(ins["L0_attn_W_re"])
    W_im = np.asarray(ins["L0_attn_W_im"])
    norm = np.asarray(ins["L0_attn_norm_scale"]).reshape(32, 8)
    W_out = np.asarray(ins["L0_attn_W_out"])

    hh, kk = np.meshgrid(np.arange(8), np.arange(32), indexing="ij")
    perm = (kk * 8 + hh).reshape(-1)
    W_mem_p = np.concatenate([W_mem[:, :256][:, perm], W_mem[:, 256:]], axis=1)
    conv_p = np.concatenate([conv_k[:, 0, :256][:, perm], conv_k[:, 0, 256:]], axis=1)
    wmem_t = np.ascontiguousarray(W_mem_p.reshape(DC, 128, 288).transpose(1, 0, 2).reshape(128, DC * 288))
    convk_t = np.zeros((384, 4), np.float32)
    convk_t[0:128] = conv_p.T[0:128]
    convk_t[128:256] = conv_p.T[128:256]
    convk_t[256:288] = conv_p.T[256:288]
    newcols = np.zeros(512, np.int64)
    for ri in range(2):
        for m in range(8):
            for h in range(8):
                for kq in range(4):
                    j = ri * 256 + m * 32 + h * 4 + kq
                    newcols[j] = ((kq * 8 + h) * 8 + m) * 2 + ri
    W_q_p = W_q[:, newcols]
    wq_t = np.ascontiguousarray(W_q_p.reshape(DC, 128, 512).transpose(1, 0, 2).reshape(128, DC * 512))
    theta_t = np.zeros((256, 8), np.float32)
    for half in range(2):
        for hp in range(4):
            for k in range(32):
                theta_t[128 * half + 32 * hp + k] = theta[k, 4 * half + hp, :]
    wr_t = np.zeros((256, 48), np.float32)
    wi_t = np.zeros((256, 48), np.float32)
    norm_t = np.zeros((256, 1), np.float32)
    for half in range(2):
        for hp in range(4):
            for k in range(32):
                r = 128 * half + 32 * hp + k
                wr_t[r] = W_re[k, 4 * half + hp]
                wi_t[r] = W_im[k, 4 * half + hp]
                norm_t[r, 0] = norm[k, 4 * half + hp]
    slopes = np.concatenate([np.asarray(ins["L0_attn_decay_slopes"]),
                             np.asarray(ins["L0_attn_anchor_slopes"])]).reshape(32, 1).astype(np.float32)
    ss = np.asarray(ins["L0_attn_score_scale"]).reshape(32, 1).astype(np.float32)
    return {"wmem": wmem_t, "wq": wq_t, "convk": convk_t, "theta": theta_t,
            "slopes": slopes, "ss": ss, "wr": wr_t, "wi": wi_t, "norm": norm_t,
            "wout": np.ascontiguousarray(W_out)}


def ref_attn_half(ins, b):
    import jax
    import jax.numpy as jnp
    from jax import lax
    x = jnp.asarray(ins["L0_attn_x"])[b:b + 1]

    def _layernorm(x, axis=-1, eps=1e-5):
        m = jnp.mean(x, axis=axis, keepdims=True)
        v = jnp.var(x, axis=axis, keepdims=True)
        return (x - m) / jnp.sqrt(v + eps)
    K_, KQ, M_, H_ = (32, 4, 8, 8)
    MEM = K_ * H_
    FLAT = K_ * H_ * M_
    CONV_W = 4

    def fwd(x, W_mem, conv_k, W_q, theta, decay_slopes, anchor_slopes, score_scale, W_re, W_im, norm_scale, W_out):
        b_, l, d = x.shape
        z = x @ W_mem
        z = lax.conv_general_dilated(z, conv_k, window_strides=(1,), padding=[(CONV_W - 1, 0)],
                                     dimension_numbers=('NHC', 'HIO', 'NHC'), feature_group_count=MEM + K_)
        k_val = z[..., :MEM].reshape(b_, l, K_, H_)
        s_raw = z[..., MEM:].reshape(b_, l, K_, 1)
        q = (x @ W_q).reshape(b_, l, KQ, H_, M_, 2)
        q_re, q_im = (q[..., 0], q[..., 1])
        pos = jnp.arange(l, dtype=jnp.float32)
        slopes_d = jax.nn.softplus(decay_slopes).reshape(1, 1, -1, 1)
        slopes_a = jax.nn.softplus(anchor_slopes).reshape(1, 1, -1, 1)
        dist = jnp.maximum(jnp.float32(l - 1) - pos, 0.0)
        log_w = jnp.concatenate([-slopes_d * dist[None, :, None, None], -slopes_a * pos[None, :, None, None]], axis=2)
        log_p = jnp.clip(score_scale[None, None, :, None] * s_raw, -20.0, 20.0)
        p_w = jnp.exp(log_p + log_w)
        phi = k_val[..., None] * theta
        re_k, im_k = (jnp.cos(phi), jnp.sin(phi))
        pw5 = p_w[..., None]
        merged = jnp.concatenate([p_w.squeeze(-1), (pw5 * re_k).reshape(b_, l, FLAT),
                                  (pw5 * im_k).reshape(b_, l, FLAT)], axis=-1)
        cs = jnp.cumsum(merged, axis=1)
        den = cs[..., :K_]
        num_re = cs[..., K_:K_ + FLAT]
        num_im = cs[..., K_ + FLAT:]
        inv_den = 1.0 / jnp.maximum(den[..., None], 0.0001)
        state_re = num_re.reshape(b_, l, K_, H_, M_) * inv_den[..., None]
        state_im = num_im.reshape(b_, l, K_, H_, M_) * inv_den[..., None]
        n_rep = K_ // KQ
        q_re = jnp.repeat(q_re, n_rep, axis=2)
        q_im = jnp.repeat(q_im, n_rep, axis=2)
        out_re = jnp.sum(state_re * q_re + state_im * q_im, axis=-1)
        out_im = jnp.sum(state_im * q_re - state_re * q_im, axis=-1)
        out_re = out_re * norm_scale
        out_im = out_im * norm_scale
        y = jnp.einsum('blkh,khn->blkn', out_re, W_re) + jnp.einsum('blkh,khn->blkn', out_im, W_im)
        y_val, y_gate = jnp.split(y, 2, axis=-1)
        y = (y_val * jax.nn.silu(y_gate)).reshape(b_, l, -1)
        return y @ W_out
    ao = fwd(_layernorm(x), jnp.asarray(ins["L0_attn_W_mem"]), jnp.asarray(ins["L0_attn_conv_k"]),
             jnp.asarray(ins["L0_attn_W_q"]), jnp.asarray(ins["L0_attn_theta"]),
             jnp.asarray(ins["L0_attn_decay_slopes"]), jnp.asarray(ins["L0_attn_anchor_slopes"]),
             jnp.asarray(ins["L0_attn_score_scale"]), jnp.asarray(ins["L0_attn_W_re"]),
             jnp.asarray(ins["L0_attn_W_im"]), jnp.asarray(ins["L0_attn_norm_scale"]),
             jnp.asarray(ins["L0_attn_W_out"]))
    return np.asarray(x + _layernorm(ao))[0]




def host_layouts_ffn(ins):
    w1g = np.asarray(ins["L0_ffn_w_gate"])
    w1u = np.asarray(ins["L0_ffn_w_up"])
    w2 = np.ascontiguousarray(np.asarray(ins["L0_ffn_w_down"]))
    w1g_t = np.ascontiguousarray(w1g.reshape(DC, 128, OT, 128).transpose(2, 1, 0, 3))
    w1u_t = np.ascontiguousarray(w1u.reshape(DC, 128, OT, 128).transpose(2, 1, 0, 3))
    sg_t = np.ascontiguousarray(np.asarray(ins["L0_ffn_s_gate"]).reshape(OT, 128).T)
    su_t = np.ascontiguousarray(np.asarray(ins["L0_ffn_s_up"]).reshape(OT, 128).T)
    sd_t = np.ascontiguousarray(np.asarray(ins["L0_ffn_s_down"]).reshape(1, D))
    return {"w1g": w1g_t, "w1u": w1u_t, "w2": w2, "sg": sg_t, "su": su_t, "sd": sd_t}


_NC_CACHE = [None]


def kernel(**inputs):
    ins = inputs
    lay = host_layouts(ins)
    lay.update(host_layouts_ffn(ins))
    if _NC_CACHE[0] is None:
        _NC_CACHE[0] = build()
    nc = _NC_CACHE[0]
    xs = np.asarray(ins["L0_attn_x"], np.float32)
    in_maps = []
    for b in range(8):
        feed = {"x": np.ascontiguousarray(xs[b])}
        feed.update(lay)
        in_maps.append(feed)
    res = run_bass_kernel_spmd(nc, in_maps, core_ids=list(range(8)))
    return np.stack([np.asarray(res.results[b]["out"], np.float32) for b in range(8)], axis=0)


# version 05 (final)
# speedup vs baseline: 1.0550x; 1.0550x over previous
import sys
import time
import numpy as np
import ml_dtypes
import concourse.bass as bass
import concourse.mybir as mybir
from concourse.bass_utils import run_bass_kernel_spmd
from contextlib import ExitStack

F32 = mybir.dt.float32
BF16 = mybir.dt.bfloat16
I32 = mybir.dt.int32
AF = mybir.ActivationFunctionType
OP = mybir.AluOpType
AX = mybir.AxisListType

L = 2048
D = 2048
DC = 16
NLT = 16
K = 32
EPS = 1e-5
TWOPI = float(2 * np.pi)
TQ = 512
QN = L // TQ
INTER = 8192
OT = INTER // 128
TC = 512
LCN = L // TC


STAGE = 4
NOXN2 = NOTRP = NOEV = False


def build():
    nc = bass.Bass("TRN2", target_bir_lowering=False)
    x_d = nc.dram_tensor("x", [L, D], F32, kind="ExternalInput").ap()
    wmem_d = nc.dram_tensor("wmem", [128, DC * 288], F32, kind="ExternalInput").ap()
    wq_d = nc.dram_tensor("wq", [128, DC * 512], F32, kind="ExternalInput").ap()
    convk_d = nc.dram_tensor("convk", [3 * 128, 4], F32, kind="ExternalInput").ap()
    theta_d = nc.dram_tensor("theta", [2 * 128, 8], F32, kind="ExternalInput").ap()
    slopes_d = nc.dram_tensor("slopes", [K, 1], F32, kind="ExternalInput").ap()
    ss_d = nc.dram_tensor("ss", [K, 1], F32, kind="ExternalInput").ap()
    wr_d = nc.dram_tensor("wr", [2 * 128, 48], F32, kind="ExternalInput").ap()
    wi_d = nc.dram_tensor("wi", [2 * 128, 48], F32, kind="ExternalInput").ap()
    norm_d = nc.dram_tensor("norm", [2 * 128, 1], F32, kind="ExternalInput").ap()
    wout_d = nc.dram_tensor("wout", [768, D], F32, kind="ExternalInput").ap()
    x1_d = nc.dram_tensor("x1", [L, D], F32).ap()
    out_d = nc.dram_tensor("out", [L, D], F32, kind="ExternalOutput").ap()
    w1g_d = nc.dram_tensor("w1g", [OT, 128, DC, 128], mybir.dt.int8, kind="ExternalInput").ap()
    w1u_d = nc.dram_tensor("w1u", [OT, 128, DC, 128], mybir.dt.int8, kind="ExternalInput").ap()
    w2_d = nc.dram_tensor("w2", [INTER, D], mybir.dt.int8, kind="ExternalInput").ap()
    sg_d = nc.dram_tensor("sg", [128, OT], F32, kind="ExternalInput").ap()
    su_d = nc.dram_tensor("su", [128, OT], F32, kind="ExternalInput").ap()
    sd_d = nc.dram_tensor("sd", [1, D], F32, kind="ExternalInput").ap()

    q_scr = nc.dram_tensor("q_scr", [512, L], F32).ap()
    pw_scr = nc.dram_tensor("pw_scr", [K, L], F32).ap()
    id_scr = nc.dram_tensor("id_scr", [K, L], F32).ap()
    xn2_scr = nc.dram_tensor("xn2_scr", [L, D], mybir.dt.bfloat16).ap()

    identN = np.eye(128, dtype=ml_dtypes.bfloat16)
    ident_h = nc.inline_tensor(identN, name="identc")
    nident_h = nc.inline_tensor(np.asarray(-identN, dtype=ml_dtypes.bfloat16), name="nidentc")
    ar = np.arange(L, dtype=np.float32)
    dist_np = np.concatenate([np.tile((L - 1) - ar, (28, 1)), np.tile(ar, (4, 1))]).astype(np.float32)
    dist_h = nc.inline_tensor(dist_np, name="distc")
    maskk_np = (np.arange(128)[:, None] % 32 == np.arange(32)[None, :]).astype(np.float32)
    maskk_h = nc.inline_tensor(maskk_np, name="maskkc")

    with ExitStack() as ctx:
        def sb_(name, shape, dt_):
            return ctx.enter_context(nc.sbuf_tensor(name, shape, dt_))
        xT = sb_("xT", [128, DC * L], BF16)
        kval = sb_("kval", [128, 2 * L], F32)
        wob = sb_("wob", [128, 6 * 2048], BF16)
        wbd = sb_("wbd", [128, 4 * 1536], BF16)
        ax = sb_("ax", [128, 2 * D], F32)
        B = sb_("B", [128, 16384], F32)
        qst = B[:, 0:2048]
        zpads = [B[:, 2176:4228], B[:, 4352:6404], B[:, 6528:8580]]
        pwt = B[:, 8704:10752]
        dent = B[:, 10752:12800]
        dist_sb = B[:, 12800:14848]
        lwt = dist_sb
        wbdf = B[:, 0:1536]

        def b32(i):
            return B[:, 512 * i:512 * (i + 1)]
        T_psi = b32(0); T_kf = b32(1); T_r = b32(2); T_rc = b32(3)
        T_msk = b32(1)
        T_sin = b32(10); T_cos = b32(12)
        T_mre = [b32(6), b32(7)]
        T_mim = [b32(8), b32(9)]
        T_nre = [b32(4), b32(11)]
        T_nim = [b32(5), b32(13)]
        T_qre = [b32(14), b32(15)]
        T_qim = [b32(16), b32(17)]
        T_pwr = b32(18)
        T_idr = b32(19)
        P_bf = B[:, 512 * 20:512 * 24].bitcast(BF16)

        def pbf(i):
            return P_bf[:, 512 * i:512 * (i + 1)]
        T_ore = [B[:, 512 * 24 + 256 * i:512 * 24 + 256 * (i + 1)].bitcast(BF16) for i in range(4)]
        yswi = B[:, 512 * 26:512 * 29].bitcast(BF16)

        def yswi_t(j):
            return yswi[:, 512 * j:512 * (j + 1)]
        T_int = B[:, 512 * 29:512 * 30].bitcast(I32)
        T_carry = B[:, 512 * 30:512 * 30 + 40]
        sqout = B[:, 512 * 31:512 * 32]
        T_sg = b32(1)
        T_t1 = b32(0)

        wmem_sb = wbd[:, 0:DC * 288]
        wq_sb = wob[:, 0:DC * 512]
        convk_sb = sb_("convk_sb", [128, 12], F32)
        theta_sb = sb_("theta_sb", [128, 16], F32)
        theta_ss = sb_("theta_ss", [128, 16], F32)
        slop = sb_("slop", [K, 8], F32)
        ss_sb = sb_("ss_sb", [K, 1], F32)
        wr_sb = sb_("wr_sb", [128, 96], F32)
        wi_sb = sb_("wi_sb", [128, 96], F32)
        norm_sb = sb_("norm_sb", [128, 2], F32)
        maskk_sb = sb_("maskk_sb", [128, 32], F32)
        ident_sb = sb_("ident_sb", [128, 128], BF16)
        nident_sb = sb_("nident_sb", [128, 128], BF16)
        ones32 = sb_("ones32", [K, 1], F32)
        eps_sb = sb_("eps_sb", [128, 1], F32)
        spc = sb_("spc", [128, 1], F32)
        spc2 = sb_("spc2", [128, 1], F32)
        xnb = sb_("xnb", [128, 2 * D], BF16)
        sqpart = sb_("sqpart", [128, 4 * NLT], F32)
        sqparta = sb_("sqparta", [128, 4 * NLT], F32)
        sqpartb = sb_("sqpartb", [128, 4 * NLT], F32)
        stats = sb_("stats", [128, 18 * NLT], F32)

        def stat(j, lt):
            return stats[:, j * NLT + lt:j * NLT + lt + 1]
        msq = sb_("msq", [128, 2], F32)

        fuseF = B[:].bitcast(BF16)
        w1g_sb = wob[:, 0:4096]
        w1u_sb = wob[:, 4096:8192]
        w2f_sb = wob[:, 8192:12288]
        x1f = ax
        y2sb = kval
        sd_rep = wbd[:, 0:4096].bitcast(F32)
        tmp2F = xnb[:, 0:1024].bitcast(F32)
        fsqout = xnb[:, 1024:2048].bitcast(F32)
        fsig = xnb[:, 2048:4096]
        sg_sb = stats[:, 6 * NLT:6 * NLT + OT]
        su_sb = stats[:, 10 * NLT:10 * NLT + OT]
        sgu_sb = stats[:, 14 * NLT:14 * NLT + OT]
        psA = ctx.enter_context(nc.psum_tensor("psA", [128, 2048], F32))
        psB = ctx.enter_context(nc.psum_tensor("psB", [128, 2048], F32))
        trp = [psB[:, 0:64].bitcast(BF16), psB[:, 512:576].bitcast(BF16)]
        og = [psB[:, 0:512], psB[:, 512:1024], psB[:, 1024:1536], psB[:, 1536:2048]]
        yv_ps = psA[:, 0:512]
        yg_ps = psA[:, 512:1024]
        gate_ps = [psA[:, 0:512], psA[:, 1024:1536]]
        up_ps = [psA[:, 512:1024], psA[:, 1536:2048]]

        sems = {}
        for s in ["sxl", "ssq0", "svar0", "sstd0", "sxn0", "strp0", "stre0",
                  "sldc", "sgl", "stzq", "szq", "sqst", "scnv", "spw0", "spw",
                  "sden", "sbc", "swbd", "sqrp", "sphr", "ssin", "sphi", "sscn",
                  "sprod", "sprd2", "ssmc", "sog", "soev", "syv", "ssg", "syw",
                  "sao", "saoe", "ssqa", "sqba", "svara", "sstda",
                  "ssqb", "sqbb", "svarb", "sstdb",
                  "sxr", "sx1", "sx1st", "sx2n", "strp2", "stre2", "ssp", "sacc", "strn", "spir",
                  "fsld", "fx1", "fw1", "fw2", "ftg", "ffu", "fdn", "fy2", "fot", "fst",
                  "fsgm", "fsq", "fqb", "fvr", "fsd"]:
            sems[s] = ctx.enter_context(nc.semaphore(s))
        S = sems
        block = ctx.enter_context(nc.Block())


        def _ffn_sync(sync):
            sync.wait_ge(S["sx1st"], 16 * NLT)
            sync.wait_ge(S["stre2"], 16 * NLT)
            sync.dma_start(out=sg_sb, in_=sg_d).then_inc(S["fsld"], 16)
            sync.dma_start(out=su_sb, in_=su_d).then_inc(S["fsld"], 16)
            sync.dma_start(out=sd_rep,
                           in_=sd_d[0].unsqueeze(0).broadcast_to([128, D])).then_inc(S["fsld"], 16)
            for lt in range(2):
                sync.dma_start(out=x1f[:, (lt % 2) * D:(lt % 2 + 1) * D],
                               in_=x1_d[128 * lt:128 * (lt + 1), :]).then_inc(S["fx1"], 16)
            for lt in range(NLT):
                sync.wait_ge(S["fot"], lt + 1)
                sync.dma_start(out=out_d[128 * lt:128 * (lt + 1), :],
                               in_=y2sb[:, (lt % 2) * D:(lt % 2 + 1) * D]).then_inc(S["fst"], 16)
                if lt + 2 < NLT:
                    sync.dma_start(out=x1f[:, (lt % 2) * D:(lt % 2 + 1) * D],
                                   in_=x1_d[128 * (lt + 2):128 * (lt + 3), :]).then_inc(S["fx1"], 16)
            sync.wait_ge(S["fst"], 16 * NLT)

        def _ffn_gp(gp):
            gp.wait_ge(S["sao"], NLT)

            def w1_pair(lc2, ot):
                got = lc2 * OT + ot
                if got >= 2:
                    gp.wait_ge(S["ftg"], 2 * (got - 1))
                gp.dma_start(out=w1g_sb[:, (got % 2) * 2048:(got % 2 + 1) * 2048],
                             in_=w1g_d[ot]).then_inc(S["fw1"], 16)
                gp.dma_start(out=w1u_sb[:, (got % 2) * 2048:(got % 2 + 1) * 2048],
                             in_=w1u_d[ot]).then_inc(S["fw1"], 16)

            for ot in range(OT):
                w1_pair(0, ot)
            for lc in range(LCN):
                if lc >= 1:
                    for ot in range(2, OT):
                        w1_pair(lc, ot)
                for pz in range(2):
                    for i in range(OT):
                        g = (lc * 2 + pz) * OT + i
                        if g >= 2:
                            gp.wait_ge(S["fdn"], g - 1)
                        gp.dma_start(out=w2f_sb[:, (g % 2) * 2048:(g % 2 + 1) * 2048],
                                     in_=w2_d[128 * i:128 * (i + 1), :]).then_inc(S["fw2"], 16)
                        if lc + 1 < LCN and pz == 1 and i in (40, 48):
                            w1_pair(lc + 1, (i - 40) // 8)

        def _ffn_t(t):
            t.wait_ge(S["strn"], 16 * DC)
            for lc in range(LCN):
                if lc >= 1:
                    t.wait_ge(S["fy2"], 2 * lc)
                for ot in range(OT):
                    got = lc * OT + ot
                    if got >= 2:
                        t.wait_ge(S["ffu"], got - 1)
                    t.wait_ge(S["fw1"], 32 * (got + 1))
                    bb = got % 2
                    for k in range(DC):
                        mm = t.matmul(gate_ps[bb],
                                      w1g_sb[:, bb * 2048 + 128 * k:bb * 2048 + 128 * (k + 1)],
                                      xT[:, k * L + lc * TC:k * L + lc * TC + TC],
                                      start=(k == 0), stop=(k == DC - 1))
                    mm.then_inc(S["ftg"], 1)
                    for k in range(DC):
                        mm = t.matmul(up_ps[bb],
                                      w1u_sb[:, bb * 2048 + 128 * k:bb * 2048 + 128 * (k + 1)],
                                      xT[:, k * L + lc * TC:k * L + lc * TC + TC],
                                      start=(k == 0), stop=(k == DC - 1))
                    mm.then_inc(S["ftg"], 1)
                t.wait_ge(S["ffu"], OT * (lc + 1))
                for pz in range(2):
                    if lc * 2 + pz >= 1:
                        t.wait_ge(S["fy2"], lc * 2 + pz)
                    for i in range(OT):
                        g = (lc * 2 + pz) * OT + i
                        t.wait_ge(S["fw2"], 16 * (g + 1))
                        for lt01 in range(2):
                            ltl = pz * 2 + lt01
                            dst = psB if lt01 == 0 else psA
                            for dc4 in range(4):
                                mm = t.matmul(dst[:, 512 * dc4:512 * (dc4 + 1)],
                                              fuseF[:, i * TC + ltl * 128:i * TC + ltl * 128 + 128],
                                              w2f_sb[:, (g % 2) * 2048 + 512 * dc4:(g % 2) * 2048 + 512 * (dc4 + 1)],
                                              start=(i == 0), stop=(i == OT - 1))
                        mm.then_inc(S["fdn"], 1)

        def _ffn_s(s):
            for lc in range(LCN):
                for ot in range(OT):
                    got = lc * OT + ot
                    s.wait_ge(S["ftg"], 2 * got + 1)
                    s.activation(fsig[:, (got % 2) * 512:(got % 2 + 1) * 512],
                                 gate_ps[got % 2], AF.Sigmoid,
                                 scale=sg_sb[:, ot:ot + 1]).then_inc(S["fsgm"], 1)
                for pz in range(2):
                    for lt01 in range(2):
                        lt = lc * 4 + pz * 2 + lt01
                        y2v = y2sb[:, lt01 * D:(lt01 + 1) * D]
                        s.wait_ge(S["fsq"], lt + 1)
                        for c4 in range(4):
                            act = s.activation(fsqout[:], y2v[:, 512 * c4:512 * (c4 + 1)], AF.Square,
                                               accum_out=sqpart[:, 4 * lt + c4:4 * lt + c4 + 1])
                        act.then_inc(S["fqb"], 1)
                        s.wait_ge(S["fvr"], lt + 1)
                        s.activation(stat(4, lt), stat(3, lt), AF.Sqrt, bias=eps_sb[:, 0:1])
                        s.activation(fsqout[:, 0:1], spc[:], AF.Sqrt).then_inc(S["fsd"], 1)

        def _ffn_v(v):
            v.wait_ge(S["fsld"], 32)
            v.tensor_mul(sgu_sb, sg_sb, su_sb)
            for lc in range(LCN):
                for ot in range(OT):
                    got = lc * OT + ot
                    bb = got % 2
                    v.wait_ge(S["ftg"], 2 * got + 2)
                    v.wait_ge(S["fsgm"], got + 1)
                    v.tensor_copy(spc[:], spc[:])
                    v.tensor_tensor(tmp2F, gate_ps[bb], fsig[:, bb * 512:(bb + 1) * 512], OP.mult)
                    v.tensor_tensor(tmp2F, tmp2F, up_ps[bb], OP.mult)
                    v.tensor_scalar(fuseF[:, ot * TC:(ot + 1) * TC], tmp2F,
                                    sgu_sb[:, ot:ot + 1], None, OP.mult).then_inc(S["ffu"], 1)
                for pz in range(2):
                    v.wait_ge(S["fdn"], 64 * (lc * 2 + pz + 1))
                    v.tensor_copy(spc[:], spc[:])
                    v.tensor_copy(spc[:], spc[:])
                    for lt01 in range(2):
                        lt = lc * 4 + pz * 2 + lt01
                        srcp = psB if lt01 == 0 else psA
                        y2v = y2sb[:, lt01 * D:(lt01 + 1) * D]
                        if lt >= 2:
                            v.wait_ge(S["fst"], 16 * (lt - 1))
                            v.wait_ge(S["fqb"], lt - 1)
                        mmv = v.tensor_tensor(y2v, srcp[:], sd_rep, OP.mult)
                        if lt01 == 1:
                            mmv.then_inc(S["fy2"], 1)
                        v.reduce_sum(stat(0, lt), y2v, axis=AX.X).then_inc(S["fsq"], 1)
                    for lt01 in range(2):
                        lt = lc * 4 + pz * 2 + lt01
                        y2v = y2sb[:, lt01 * D:(lt01 + 1) * D]
                        v.wait_ge(S["fqb"], lt + 1)
                        v.reduce_sum(stat(1, lt), sqpart[:, 4 * lt:4 * lt + 4], axis=AX.X)
                        v.tensor_scalar(stat(2, lt), stat(0, lt), 1.0 / D, None, OP.mult)
                        v.tensor_scalar(stat(3, lt), stat(1, lt), 1.0 / D, None, OP.mult)
                        v.tensor_mul(msq[:, 0:1], stat(2, lt), stat(2, lt))
                        v.tensor_copy(spc[:], spc[:])
                        v.tensor_sub(stat(3, lt), stat(3, lt), msq[:, 0:1]).then_inc(S["fvr"], 1)
                        v.wait_ge(S["fsd"], lt + 1)
                        v.tensor_copy(spc[:], spc[:])
                        v.reciprocal(stat(5, lt), stat(4, lt))
                        v.tensor_copy(spc[:], spc[:])
                        v.tensor_scalar(y2v, y2v, stat(2, lt), stat(5, lt),
                                        OP.subtract, OP.mult)
                        v.wait_ge(S["fx1"], 16 * (lt + 1))
                        v.tensor_add(y2v, y2v, x1f[:, (lt % 2) * D:(lt % 2 + 1) * D]).then_inc(S["fot"], 1)

        NCONST = 16

        def jidx(tc, m, half):
            return (tc * 8 + m) * 2 + half

        @block.sync
        def _(sync):
            sync.dma_start(out=convk_sb[:, 0:4], in_=convk_d[0:128, :]).then_inc(S["sldc"], 16)
            sync.dma_start(out=convk_sb[:, 4:8], in_=convk_d[128:256, :]).then_inc(S["sldc"], 16)
            sync.dma_start(out=convk_sb[:, 8:12], in_=convk_d[256:384, :]).then_inc(S["sldc"], 16)
            sync.dma_start(out=theta_sb[:, 0:8], in_=theta_d[0:128, :]).then_inc(S["sldc"], 16)
            sync.dma_start(out=theta_sb[:, 8:16], in_=theta_d[128:256, :]).then_inc(S["sldc"], 16)
            sync.dma_start(out=slop[:, 0:1], in_=slopes_d).then_inc(S["sldc"], 16)
            sync.dma_start(out=ss_sb[:], in_=ss_d).then_inc(S["sldc"], 16)
            sync.dma_start(out=wr_sb[:, 0:48], in_=wr_d[0:128, :]).then_inc(S["sldc"], 16)
            sync.dma_start(out=wr_sb[:, 48:96], in_=wr_d[128:256, :]).then_inc(S["sldc"], 16)
            sync.dma_start(out=wi_sb[:, 0:48], in_=wi_d[0:128, :]).then_inc(S["sldc"], 16)
            sync.dma_start(out=wi_sb[:, 48:96], in_=wi_d[128:256, :]).then_inc(S["sldc"], 16)
            sync.dma_start(out=norm_sb[:, 0:1], in_=norm_d[0:128, :]).then_inc(S["sldc"], 16)
            sync.dma_start(out=norm_sb[:, 1:2], in_=norm_d[128:256, :]).then_inc(S["sldc"], 16)
            sync.dma_start(out=ident_sb[:], in_=ident_h.ap()).then_inc(S["sldc"], 16)
            sync.dma_start(out=nident_sb[:], in_=nident_h.ap()).then_inc(S["sldc"], 16)
            sync.dma_start(out=maskk_sb[:], in_=maskk_h.ap()).then_inc(S["sldc"], 16)
            sync.dma_start(out=dist_sb[0:32, :], in_=dist_h.ap()).then_inc(S["sldc"], 16)
            for lt in range(NLT):
                if lt >= 2:
                    sync.wait_ge(S["sxn0"], lt - 1)
                    sync.wait_ge(S["ssq0"], lt - 1)
                sync.dma_start(out=ax[:, (lt % 2) * D:(lt % 2 + 1) * D],
                               in_=x_d[128 * lt:128 * (lt + 1), :]).then_inc(S["sxl"], 16)
            if STAGE == 0:
                sync.wait_ge(S["stre0"], 256)
                sync.dma_start(out=x1_d[0:128, :], in_=ax[:, 0:D]).then_inc(S["sx1st"], 16)
                sync.wait_ge(S["sx1st"], 16)
                return
            for qt in range(4):
                sync.wait_ge(S["szq"], 4 + qt)
                sync.dma_start(out=q_scr[128 * qt:128 * (qt + 1), :], in_=qst).then_inc(S["sqst"], 16)
            sync.wait_ge(S["spw"], 1)
            sync.dma_start(out=pw_scr, in_=pwt[0:32, :]).then_inc(S["sbc"], 16)
            sync.wait_ge(S["sden"], 1)
            sync.dma_start(out=id_scr, in_=dent[0:32, :]).then_inc(S["sbc"], 16)
            if STAGE == 1:
                sync.wait_ge(S["sbc"], 32)
                sync.wait_ge(S["swbd"], 4)
                sync.wait_ge(S["sgl"], 128)
                sync.dma_start(out=x1_d[0:128, :], in_=ax[:, 0:D]).then_inc(S["sx1st"], 16)
                sync.wait_ge(S["sx1st"], 16)
                return
            for tc in range(QN):
                t0 = TQ * tc
                sync.wait_ge(S["sbc"], 32)
                if tc >= 1:
                    sync.wait_ge(S["sphi"], 16 * tc)
                    sync.wait_ge(S["soev"], tc)
                sync.dma_start(out=T_pwr,
                               in_=pw_scr[:, t0:t0 + TQ].unsqueeze(0).broadcast_to([4, K, TQ])).then_inc(S["sqrp"], 16)
                sync.dma_start(out=T_idr,
                               in_=id_scr[:, t0:t0 + TQ].unsqueeze(0).broadcast_to([4, K, TQ])).then_inc(S["sqrp"], 16)
                for m in range(8):
                    for half in range(2):
                        j = jidx(tc, m, half)
                        if j >= 2:
                            sync.wait_ge(S["sprod"], j - 1)
                        base = m * 32 + 16 * half
                        for ri in range(2):
                            dst = (T_qre if ri == 0 else T_qim)[half]
                            srows = q_scr[256 * ri + base:256 * ri + base + 16, t0:t0 + TQ]
                            src = srows.rearrange("(h q) t -> h q t", q=4).unsqueeze(2)
                            src = src.broadcast_to([4, 4, 8, TQ])
                            sync.dma_start(out=dst, in_=src).then_inc(S["sqrp"], 16)
                if STAGE >= 4:
                    for l4 in range(4):
                        lt = tc * 4 + l4
                        sync.wait_ge(S["sacc"], 16 * (lt + 1))
                        sync.dma_start(out=x1_d[128 * lt:128 * (lt + 1), :],
                                       in_=ax[:, (lt % 2) * D:(lt % 2 + 1) * D]).then_inc(S["sx1st"], 16)
                        sync.wait_ge(S["sx2n"], lt + 1)
                        sync.dma_start(out=xn2_scr[128 * lt:128 * (lt + 1), :],
                                       in_=xnb[:, (lt % 2) * D:(lt % 2 + 1) * D]).then_inc(S["stre2"], 16)
            if STAGE >= 4:
                sync.wait_ge(S["sx1st"], 16 * NLT)
                sync.wait_ge(S["stre2"], 16 * NLT)
                for k in range(DC):
                    sync.dma_start_transpose(xT[:, L * k:L * (k + 1)],
                                             xn2_scr[:, 128 * k:128 * (k + 1)]).then_inc(S["strn"], 16)
                _ffn_sync(sync)
            elif STAGE == 3:
                sync.wait_ge(S["saoe"], NLT)
                sync.dma_start(out=x1_d[0:128, :], in_=ax[:, 0:D]).then_inc(S["sx1st"], 16)
                sync.wait_ge(S["sx1st"], 16)
            else:
                sync.wait_ge(S["syw"], 6 * QN)
                sync.dma_start(out=x1_d[0:128, :], in_=ax[:, 0:D]).then_inc(S["sx1st"], 16)
                sync.wait_ge(S["sx1st"], 16)

        @block.gpsimd
        def _(gp):
            gp.dma_start(out=wmem_sb[:], in_=wmem_d).then_inc(S["sgl"], 16)
            gp.dma_start(out=wq_sb[:], in_=wq_d).then_inc(S["sgl"], 16)
            if STAGE == 0:
                return
            gp.wait_ge(S["sldc"], 16 * (NCONST + 1))
            gp.wait_ge(S["sqst"], 16 * 4)
            for half in range(2):
                for ri, wsrc in [(0, wr_sb), (1, wi_sb)]:
                    for k in range(K):
                        gp.tensor_scalar(wbdf[:, 24 * k:24 * (k + 1)],
                                         wsrc[:, 48 * half:48 * half + 24],
                                         maskk_sb[:, k:k + 1], None, OP.mult)
                        gp.tensor_scalar(wbdf[:, 768 + 24 * k:768 + 24 * (k + 1)],
                                         wsrc[:, 48 * half + 24:48 * half + 48],
                                         maskk_sb[:, k:k + 1], None, OP.mult)
                    idx = half * 2 + ri
                    gp.tensor_scalar(wbd[:, 1536 * idx:1536 * (idx + 1)], wbdf[:],
                                     norm_sb[:, half:half + 1], None, OP.mult).then_inc(S["swbd"], 1)
            for j in range(6):
                gp.dma_start(out=wob[:, 2048 * j:2048 * (j + 1)],
                             in_=wout_d[128 * j:128 * (j + 1), :]).then_inc(S["sgl"], 16)
            gp.wait_ge(S["szq"], 3)
            for ct in range(3):
                dst = [kval[:, 0:L], kval[:, L:2 * L], pwt][ct]
                zp = zpads[ct]
                gp.tensor_scalar(dst, zp[:, 1:1 + L], convk_sb[:, 4 * ct:4 * ct + 1], None, OP.mult)
                for w in range(1, 4):
                    gp.tensor_scalar(dent, zp[:, w + 1:w + 1 + L],
                                     convk_sb[:, 4 * ct + w:4 * ct + w + 1], None, OP.mult)
                    gp.tensor_tensor(dst, dst, dent, OP.add)
                gp.tensor_copy(spc2[:], spc2[:]).then_inc(S["scnv"], 1)
            if STAGE < 2:
                return
            for tc in range(QN):
                for m in range(8):
                    for half in range(2):
                        j = jidx(tc, m, half)
                        jb = j % 2
                        gp.wait_ge(S["sscn"], j + 1)
                        c = 4 * m + 2 * half
                        if tc < QN - 1:
                            gp.tensor_copy(T_carry[:, c:c + 1], T_nre[jb][:, TQ - 1:TQ])
                            gp.tensor_copy(T_carry[:, c + 1:c + 2], T_nim[jb][:, TQ - 1:TQ])
                        gp.tensor_copy(spc2[:], spc2[:])
                        gp.tensor_copy(spc2[:], spc2[:]).then_inc(S["sprd2"], 1)
                if STAGE >= 4:
                    for l4 in range(4):
                        lt = tc * 4 + l4
                        gp.wait_ge(S["sxr"], lt + 1)
                        gp.dma_start(out=ax[:, (lt % 2) * D:(lt % 2 + 1) * D],
                                     in_=x_d[128 * lt:128 * (lt + 1), :],
                                     accum_op=OP.add).then_inc(S["sacc"], 16)

            _ffn_gp(gp)

        @block.tensor
        def _(t):
            for lt in range(NLT):
                t.wait_ge(S["sxn0"], lt + 1)
                for k in range(DC):
                    g = lt * DC + k
                    if g >= 2:
                        t.wait_ge(S["stre0"], g - 1)
                    t.transpose(trp[g % 2][:, 0:128],
                                xnb[:, (lt % 2) * D + 128 * k:(lt % 2) * D + 128 * (k + 1)],
                                ident_sb[:]).then_inc(S["strp0"], 1)
            if STAGE == 0:
                return
            t.wait_ge(S["stre0"], 256)
            t.wait_ge(S["sgl"], 32)
            for ct in range(3):
                wdt = 128 if ct < 2 else 32
                if ct >= 1:
                    t.wait_ge(S["szq"], ct)
                for tch in range(4):
                    for k in range(DC):
                        mm = t.matmul(psA[0:wdt, 512 * tch:512 * (tch + 1)],
                                      wmem_sb[:, 288 * k + 128 * ct:288 * k + 128 * ct + wdt],
                                      xT[:, L * k + 512 * tch:L * k + 512 * (tch + 1)],
                                      start=(k == 0), stop=(k == DC - 1))
                mm.then_inc(S["stzq"], 1)
            for qt in range(4):
                t.wait_ge(S["szq"], 3 + qt)
                for tch in range(4):
                    for k in range(DC):
                        mm = t.matmul(psA[:, 512 * tch:512 * (tch + 1)],
                                      wq_sb[:, 512 * k + 128 * qt:512 * k + 128 * (qt + 1)],
                                      xT[:, L * k + 512 * tch:L * k + 512 * (tch + 1)],
                                      start=(k == 0), stop=(k == DC - 1))
                mm.then_inc(S["stzq"], 1)
            if STAGE < 2:
                return
            for tc in range(QN):
                if tc >= 1:
                    t.wait_ge(S["soev"], tc)
                    if STAGE >= 4 and not NOXN2 and not NOTRP and not NOEV:
                        t.wait_ge(S["stre2"], 64 * tc)
                for m in range(8):
                    for half in range(2):
                        j = jidx(tc, m, half)
                        jb = j % 2
                        t.wait_ge(S["sprod"], j + 1)
                        st = (m == 0)
                        sp = (m == 7)
                        t.matmul(og[half], ident_sb[:], pbf(4 * jb + 0), start=st, stop=False)
                        t.matmul(og[half], ident_sb[:], pbf(4 * jb + 1), start=False, stop=sp)
                        t.matmul(og[2 + half], ident_sb[:], pbf(4 * jb + 2), start=st, stop=False)
                        mm = t.matmul(og[2 + half], ident_sb[:], pbf(4 * jb + 3), start=False, stop=sp)
                        mm.then_inc(S["ssmc"], 1)
                t.wait_ge(S["swbd"], 4)
                t.wait_ge(S["soev"], tc + 1)
                if tc >= 1:
                    t.wait_ge(S["saoe"], 4 * tc)
                for jy in range(6):
                    gy = tc * 6 + jy
                    if gy >= 1:
                        t.wait_ge(S["syw"], gy)
                    first = True
                    for half in range(2):
                        for ri in range(2):
                            idx = half * 2 + ri
                            last = (half == 1 and ri == 1)
                            t.matmul(yv_ps, wbd[:, 1536 * idx + 128 * jy:1536 * idx + 128 * (jy + 1)],
                                     T_ore[ri * 2 + half], start=first, stop=last)
                            mm = t.matmul(yg_ps, wbd[:, 1536 * idx + 768 + 128 * jy:1536 * idx + 768 + 128 * (jy + 1)],
                                          T_ore[ri * 2 + half], start=first, stop=last)
                            first = False
                    mm.then_inc(S["syv"], 1)
                if STAGE < 3:
                    continue
                t.wait_ge(S["syw"], 6 * (tc + 1))
                t.wait_ge(S["sgl"], 16 * 8)
                for l4 in range(4):
                    lt = tc * 4 + l4
                    if lt >= 1:
                        t.wait_ge(S["saoe"], lt)
                    for jy in range(6):
                        for dcc in range(4):
                            mm = t.matmul(psA[:, 512 * dcc:512 * (dcc + 1)],
                                          yswi_t(jy)[:, 128 * l4:128 * (l4 + 1)],
                                          wob[:, 2048 * jy + 512 * dcc:2048 * jy + 512 * (dcc + 1)],
                                          start=(jy == 0), stop=(jy == 5))
                    mm.then_inc(S["sao"], 1)

                continue

            _ffn_t(t)

        @block.scalar
        def _(s):
            for lt in range(NLT):
                s.wait_ge(S["sxl"], 16 * (lt + 1))
                xv = ax[:, (lt % 2) * D:(lt % 2 + 1) * D]
                for c4 in range(4):
                    act = s.activation(sqout[:], xv[:, 512 * c4:512 * (c4 + 1)], AF.Square,
                                       accum_out=sqpart[:, 4 * lt + c4:4 * lt + c4 + 1])
                act.then_inc(S["ssq0"], 1)
                s.wait_ge(S["svar0"], lt + 1)
                s.activation(stat(4, lt), stat(3, lt), AF.Sqrt,
                             bias=eps_sb[:, 0:1])
                s.activation(sqout[:, 0:1], spc[:], AF.Sqrt).then_inc(S["sstd0"], 1)
                for k in range(DC):
                    g = lt * DC + k
                    s.wait_ge(S["strp0"], g + 1)
                    s.copy(xT[:, L * k + 128 * lt:L * k + 128 * (lt + 1)],
                           trp[g % 2][:, 0:128]).then_inc(S["stre0"], 1)
            if STAGE == 0:
                return
            s.wait_ge(S["ssp"], 2)
            s.activation(slop[:, 4:5], slop[:, 2:3], AF.Exp, scale=-1.0)
            s.activation(sqout[0:K, 0:1], spc[0:K, :], AF.Exp)
            s.activation(slop[:, 5:6], slop[:, 4:5], AF.Ln, bias=ones32[:, 0:1])
            s.activation(sqout[0:K, 0:1], spc[0:K, :], AF.Exp)
            s.activation(slop[:, 6:7], slop[:, 5:6], AF.Identity, bias=slop[:, 3:4])
            s.activation(sqout[0:K, 0:1], spc[0:K, :], AF.Exp)
            s.activation(sqout[0:K, 1:2], spc[0:K, :], AF.Exp)
            s.mul(slop[:, 7:8], slop[:, 6:7], -1.0)
            s.activation(sqout[0:K, 0:1], spc[0:K, :], AF.Exp)
            s.activation(sqout[0:K, 0:1], spc[0:K, :], AF.Exp).then_inc(S["ssp"], 1)
            s.wait_ge(S["spw0"], 1)
            s.activation(pwt[0:K, :], pwt[0:K, :], AF.Exp).then_inc(S["spw"], 1)
            if STAGE < 2:
                return
            for tc in range(QN):
                for m in range(8):
                    for half in range(2):
                        j = jidx(tc, m, half)
                        s.wait_ge(S["sphr"], j + 1)
                        s.activation(T_sin, T_r, AF.Sin, scale=TWOPI)
                        s.activation(T_cos, T_rc, AF.Sin, scale=TWOPI).then_inc(S["ssin"], 1)
                for jy in range(6):
                    gy = tc * 6 + jy
                    s.wait_ge(S["syv"], gy + 1)
                    s.activation(sqout[:, 0:1], spc[:], AF.Sigmoid)
                    s.activation(T_sg, yg_ps, AF.Sigmoid).then_inc(S["ssg"], 1)
                if STAGE < 4:
                    continue
                for l4 in range(4):
                    lt = tc * 4 + l4
                    axv = ax[:, (lt % 2) * D:(lt % 2 + 1) * D]
                    s.wait_ge(S["ssqa"], lt + 1)
                    for c4 in range(4):
                        act = s.activation(sqout[:], axv[:, 512 * c4:512 * (c4 + 1)], AF.Square,
                                           accum_out=sqparta[:, 4 * lt + c4:4 * lt + c4 + 1])
                    act.then_inc(S["sqba"], 1)
                    s.wait_ge(S["svara"], lt + 1)
                    s.activation(stat(10, lt), stat(9, lt), AF.Sqrt,
                                 bias=eps_sb[:, 0:1])
                    s.activation(sqout[:, 0:1], spc[:], AF.Sqrt).then_inc(S["sstda"], 1)
                    s.wait_ge(S["ssqb"], lt + 1)
                    for c4 in range(4):
                        act = s.activation(sqout[:], axv[:, 512 * c4:512 * (c4 + 1)], AF.Square,
                                           accum_out=sqpartb[:, 4 * lt + c4:4 * lt + c4 + 1])
                    act.then_inc(S["sqbb"], 1)
                    s.wait_ge(S["svarb"], lt + 1)
                    s.activation(stat(16, lt), stat(15, lt), AF.Sqrt,
                                 bias=eps_sb[:, 0:1])
                    s.activation(sqout[:, 0:1], spc[:], AF.Sqrt).then_inc(S["sstdb"], 1)


            _ffn_s(s)

        @block.vector
        def _(v):
            v.memset(eps_sb[:], EPS)
            v.memset(spc[:], 1.0)
            v.memset(ones32[:], 1.0)
            v.memset(spc2[:], 1.0)
            for zp in zpads:
                v.memset(zp[:, 0:4], 0.0)
            v.wait_ge(S["sldc"], 16 * (NCONST + 1))
            v.tensor_scalar(theta_ss[:], theta_sb[:], float(1.0 / TWOPI), None, OP.mult)
            v.wait_ge(S["sldc"], 16 * (NCONST + 1))
            v.tensor_scalar(slop[:, 1:2], slop[:, 0:1], -1.0, None, OP.mult)
            v.tensor_scalar(slop[:, 3:4], slop[:, 0:1], 0.0, None, OP.max)
            v.tensor_tensor(slop[:, 2:3], slop[:, 0:1], slop[:, 1:2], OP.max).then_inc(S["ssp"], 2)
            for lt in range(NLT):
                xv = ax[:, (lt % 2) * D:(lt % 2 + 1) * D]
                v.wait_ge(S["sxl"], 16 * (lt + 1))
                v.reduce_sum(stat(0, lt), xv, axis=AX.X)
                v.wait_ge(S["ssq0"], lt + 1)
                v.reduce_sum(stat(1, lt), sqpart[:, 4 * lt:4 * lt + 4], axis=AX.X)
                v.tensor_scalar(stat(2, lt), stat(0, lt), 1.0 / D, None, OP.mult)
                v.tensor_scalar(stat(3, lt), stat(1, lt), 1.0 / D, None, OP.mult)
                v.tensor_mul(msq[:, 0:1], stat(2, lt), stat(2, lt))
                v.tensor_copy(spc[:], spc[:])
                v.tensor_sub(stat(3, lt), stat(3, lt), msq[:, 0:1]).then_inc(S["svar0"], 1)
                v.wait_ge(S["sstd0"], lt + 1)
                v.tensor_copy(spc[:], spc[:])
                v.reciprocal(stat(5, lt), stat(4, lt))
                if lt >= 2:
                    v.wait_ge(S["strp0"], 16 * (lt - 1))
                v.tensor_copy(spc[:], spc[:])
                v.tensor_scalar(xnb[:, (lt % 2) * D:(lt % 2 + 1) * D], xv, stat(2, lt), stat(5, lt),
                                OP.subtract, OP.mult).then_inc(S["sxn0"], 1)
            if STAGE == 0:
                return
            for ct in range(3):
                v.wait_ge(S["stzq"], ct + 1)
                v.tensor_copy(spc[:], spc[:])
                wdt = 128 if ct < 2 else 32
                v.tensor_copy(zpads[ct][0:wdt, 4:4 + L], psA[0:wdt, :]).then_inc(S["szq"], 1)
            for qt in range(4):
                v.wait_ge(S["stzq"], 4 + qt)
                v.tensor_copy(spc[:], spc[:])
                if qt >= 1:
                    v.wait_ge(S["sqst"], 16 * qt)
                v.tensor_copy(qst, psA[:]).then_inc(S["szq"], 1)
            v.wait_ge(S["scnv"], 3)
            v.tensor_scalar(pwt[0:K, :], pwt[0:K, :], ss_sb[:, 0:1], None, OP.mult)
            v.tensor_scalar(pwt[0:K, :], pwt[0:K, :], 20.0, None, OP.min)
            v.tensor_scalar(pwt[0:K, :], pwt[0:K, :], -20.0, None, OP.max)
            v.wait_ge(S["ssp"], 3)
            v.tensor_copy(spc[:], spc[:])
            v.tensor_copy(spc[:], spc[:])
            v.wait_ge(S["sldc"], 16 * (NCONST + 1))
            v.tensor_scalar(lwt[0:K, :], dist_sb[0:K, :], slop[:, 7:8], None, OP.mult)
            v.tensor_add(pwt[0:K, :], pwt[0:K, :], lwt[0:K, :]).then_inc(S["spw0"], 1)
            v.wait_ge(S["spw"], 1)
            v.tensor_tensor_scan(dent[0:K, :], pwt[0:K, :], pwt[0:K, :], 0.0, OP.add, OP.bypass)
            v.tensor_scalar(dent[0:K, :], dent[0:K, :], 1e-4, None, OP.max)
            v.reciprocal(dent[0:K, :], dent[0:K, :]).then_inc(S["sden"], 1)
            if STAGE < 2:
                return
            for tc in range(QN):
                for m in range(8):
                    for half in range(2):
                        j = jidx(tc, m, half)
                        jb = j % 2
                        if m == 0 and half == 0:
                            if tc == 0:
                                v.wait_ge(S["swbd"], 4)
                            v.wait_ge(S["sqrp"], 544 * tc + 32)
                        v.tensor_scalar(T_psi, kval[:, half * L + TQ * tc:half * L + TQ * tc + TQ],
                                        theta_ss[:, 8 * half + m:8 * half + m + 1], None, OP.mult)
                        v.tensor_copy(T_int, T_psi)
                        v.tensor_copy(T_kf, T_int)
                        v.tensor_tensor(T_r, T_psi, T_kf, OP.subtract)
                        v.tensor_scalar(T_rc, T_r, 0.25, None, OP.add)
                        v.tensor_copy(T_int, T_rc)
                        v.tensor_copy(T_kf, T_int)
                        v.tensor_tensor(T_rc, T_rc, T_kf, OP.subtract).then_inc(S["sphr"], 1)
                        v.wait_ge(S["ssin"], j + 1)
                        v.tensor_tensor(T_mre[jb], T_cos, T_pwr, OP.mult)
                        v.tensor_tensor(T_mim[jb], T_sin, T_pwr, OP.mult).then_inc(S["sphi"], 1)
                        if j >= 2:
                            v.wait_ge(S["sprd2"], j - 1)
                        c = 4 * m + 2 * half
                        ini_re = 0.0 if tc == 0 else T_carry[:, c:c + 1]
                        ini_im = 0.0 if tc == 0 else T_carry[:, c + 1:c + 2]
                        v.tensor_tensor_scan(T_nre[jb], T_mre[jb], T_mre[jb], ini_re, OP.add, OP.bypass)
                        v.tensor_tensor_scan(T_nim[jb], T_mim[jb], T_mim[jb], ini_im, OP.add, OP.bypass).then_inc(S["sscn"], 1)
                        v.wait_ge(S["sqrp"], 544 * tc + 32 + 32 * (2 * m + half + 1))
                        if j >= 2:
                            v.wait_ge(S["ssmc"], j - 1)
                        v.tensor_tensor(pbf(4 * jb + 0), T_nre[jb], T_qre[half], OP.mult)
                        v.tensor_tensor(pbf(4 * jb + 1), T_nim[jb], T_qim[half], OP.mult)
                        v.tensor_tensor(pbf(4 * jb + 2), T_nim[jb], T_qre[half], OP.mult)
                        v.tensor_tensor(pbf(4 * jb + 3), T_nre[jb], T_qim[half], OP.mult)
                        v.tensor_scalar(pbf(4 * jb + 3), pbf(4 * jb + 3), -1.0, None, OP.mult)
                        v.tensor_copy(spc[:], spc[:]).then_inc(S["sprod"], 1)
                v.wait_ge(S["ssmc"], 16 * (tc + 1))
                v.tensor_copy(spc[:], spc[:])
                v.tensor_copy(spc[:], spc[:])
                for gi in range(4):
                    v.tensor_tensor(T_ore[gi], og[gi], T_idr, OP.mult)
                v.tensor_copy(spc[:], spc[:])
                v.tensor_copy(spc[:], spc[:])
                v.tensor_copy(spc[:], spc[:])
                v.tensor_copy(spc[:], spc[:]).then_inc(S["soev"], 1)
                for jy in range(6):
                    gy = tc * 6 + jy
                    v.wait_ge(S["ssg"], gy + 1)
                    v.tensor_copy(spc[:], spc[:])
                    v.tensor_tensor(T_t1, yg_ps, T_sg, OP.mult)
                    v.tensor_tensor(yswi_t(jy), T_t1, yv_ps, OP.mult).then_inc(S["syw"], 1)
                if STAGE < 3:
                    continue
                for l4 in range(4):
                    lt = tc * 4 + l4
                    axv = ax[:, (lt % 2) * D:(lt % 2 + 1) * D]
                    v.wait_ge(S["sao"], lt + 1)
                    v.tensor_copy(spc[:], spc[:])
                    if lt >= 2 and STAGE >= 4:
                        v.wait_ge(S["sx1st"], 16 * (lt - 1))
                        v.wait_ge(S["sqbb"], lt - 1)
                    v.tensor_copy(axv, psA[:]).then_inc(S["saoe"], 1)
                    if STAGE < 4:
                        continue
                    v.reduce_sum(stat(6, lt), axv, axis=AX.X).then_inc(S["ssqa"], 1)
                    v.wait_ge(S["sqba"], lt + 1)
                    v.reduce_sum(stat(7, lt), sqparta[:, 4 * lt:4 * lt + 4], axis=AX.X)
                    v.tensor_scalar(stat(8, lt), stat(6, lt), 1.0 / D, None, OP.mult)
                    v.tensor_scalar(stat(9, lt), stat(7, lt), 1.0 / D, None, OP.mult)
                    v.tensor_mul(msq[:, 0:1], stat(8, lt), stat(8, lt))
                    v.tensor_copy(spc[:], spc[:])
                    v.tensor_sub(stat(9, lt), stat(9, lt), msq[:, 0:1]).then_inc(S["svara"], 1)
                    v.wait_ge(S["sstda"], lt + 1)
                    v.tensor_copy(spc[:], spc[:])
                    v.reciprocal(stat(11, lt), stat(10, lt))
                    v.tensor_copy(spc[:], spc[:])
                    v.tensor_scalar(axv, axv, stat(8, lt), stat(11, lt),
                                    OP.subtract, OP.mult).then_inc(S["sxr"], 1)
                    v.wait_ge(S["sacc"], 16 * (lt + 1))
                    v.reduce_sum(stat(12, lt), axv, axis=AX.X).then_inc(S["ssqb"], 1)
                    v.wait_ge(S["sqbb"], lt + 1)
                    v.reduce_sum(stat(13, lt), sqpartb[:, 4 * lt:4 * lt + 4], axis=AX.X)
                    v.tensor_scalar(stat(14, lt), stat(12, lt), 1.0 / D, None, OP.mult)
                    v.tensor_scalar(stat(15, lt), stat(13, lt), 1.0 / D, None, OP.mult)
                    v.tensor_mul(msq[:, 1:2], stat(14, lt), stat(14, lt))
                    v.tensor_copy(spc[:], spc[:])
                    v.tensor_sub(stat(15, lt), stat(15, lt), msq[:, 1:2]).then_inc(S["svarb"], 1)
                    v.wait_ge(S["sstdb"], lt + 1)
                    v.tensor_copy(spc[:], spc[:])
                    v.reciprocal(stat(17, lt), stat(16, lt))
                    if NOXN2:
                        continue
                    if lt >= 2:
                        v.wait_ge(S["stre2"], 16 * (lt - 1))
                    v.tensor_copy(spc[:], spc[:])
                    v.tensor_scalar(xnb[:, (lt % 2) * D:(lt % 2 + 1) * D], axv,
                                    stat(14, lt), stat(17, lt), OP.subtract, OP.mult).then_inc(S["sx2n"], 1)

            _ffn_v(v)

    return nc


def host_layouts(ins):
    W_mem = np.asarray(ins["L0_attn_W_mem"])
    conv_k = np.asarray(ins["L0_attn_conv_k"])
    W_q = np.asarray(ins["L0_attn_W_q"])
    theta = np.asarray(ins["L0_attn_theta"]).reshape(32, 8, 8)
    W_re = np.asarray(ins["L0_attn_W_re"])
    W_im = np.asarray(ins["L0_attn_W_im"])
    norm = np.asarray(ins["L0_attn_norm_scale"]).reshape(32, 8)
    W_out = np.asarray(ins["L0_attn_W_out"])

    hh, kk = np.meshgrid(np.arange(8), np.arange(32), indexing="ij")
    perm = (kk * 8 + hh).reshape(-1)
    W_mem_p = np.concatenate([W_mem[:, :256][:, perm], W_mem[:, 256:]], axis=1)
    conv_p = np.concatenate([conv_k[:, 0, :256][:, perm], conv_k[:, 0, 256:]], axis=1)
    wmem_t = np.ascontiguousarray(W_mem_p.reshape(DC, 128, 288).transpose(1, 0, 2).reshape(128, DC * 288))
    convk_t = np.zeros((384, 4), np.float32)
    convk_t[0:128] = conv_p.T[0:128]
    convk_t[128:256] = conv_p.T[128:256]
    convk_t[256:288] = conv_p.T[256:288]
    newcols = np.zeros(512, np.int64)
    for ri in range(2):
        for m in range(8):
            for h in range(8):
                for kq in range(4):
                    j = ri * 256 + m * 32 + h * 4 + kq
                    newcols[j] = ((kq * 8 + h) * 8 + m) * 2 + ri
    W_q_p = W_q[:, newcols]
    wq_t = np.ascontiguousarray(W_q_p.reshape(DC, 128, 512).transpose(1, 0, 2).reshape(128, DC * 512))
    theta_t = np.zeros((256, 8), np.float32)
    for half in range(2):
        for hp in range(4):
            for k in range(32):
                theta_t[128 * half + 32 * hp + k] = theta[k, 4 * half + hp, :]
    wr_t = np.zeros((256, 48), np.float32)
    wi_t = np.zeros((256, 48), np.float32)
    norm_t = np.zeros((256, 1), np.float32)
    for half in range(2):
        for hp in range(4):
            for k in range(32):
                r = 128 * half + 32 * hp + k
                wr_t[r] = W_re[k, 4 * half + hp]
                wi_t[r] = W_im[k, 4 * half + hp]
                norm_t[r, 0] = norm[k, 4 * half + hp]
    slopes = np.concatenate([np.asarray(ins["L0_attn_decay_slopes"]),
                             np.asarray(ins["L0_attn_anchor_slopes"])]).reshape(32, 1).astype(np.float32)
    ss = np.asarray(ins["L0_attn_score_scale"]).reshape(32, 1).astype(np.float32)
    return {"wmem": wmem_t, "wq": wq_t, "convk": convk_t, "theta": theta_t,
            "slopes": slopes, "ss": ss, "wr": wr_t, "wi": wi_t, "norm": norm_t,
            "wout": np.ascontiguousarray(W_out)}


def ref_attn_half(ins, b):
    import jax
    import jax.numpy as jnp
    from jax import lax
    x = jnp.asarray(ins["L0_attn_x"])[b:b + 1]

    def _layernorm(x, axis=-1, eps=1e-5):
        m = jnp.mean(x, axis=axis, keepdims=True)
        v = jnp.var(x, axis=axis, keepdims=True)
        return (x - m) / jnp.sqrt(v + eps)
    K_, KQ, M_, H_ = (32, 4, 8, 8)
    MEM = K_ * H_
    FLAT = K_ * H_ * M_
    CONV_W = 4

    def fwd(x, W_mem, conv_k, W_q, theta, decay_slopes, anchor_slopes, score_scale, W_re, W_im, norm_scale, W_out):
        b_, l, d = x.shape
        z = x @ W_mem
        z = lax.conv_general_dilated(z, conv_k, window_strides=(1,), padding=[(CONV_W - 1, 0)],
                                     dimension_numbers=('NHC', 'HIO', 'NHC'), feature_group_count=MEM + K_)
        k_val = z[..., :MEM].reshape(b_, l, K_, H_)
        s_raw = z[..., MEM:].reshape(b_, l, K_, 1)
        q = (x @ W_q).reshape(b_, l, KQ, H_, M_, 2)
        q_re, q_im = (q[..., 0], q[..., 1])
        pos = jnp.arange(l, dtype=jnp.float32)
        slopes_d = jax.nn.softplus(decay_slopes).reshape(1, 1, -1, 1)
        slopes_a = jax.nn.softplus(anchor_slopes).reshape(1, 1, -1, 1)
        dist = jnp.maximum(jnp.float32(l - 1) - pos, 0.0)
        log_w = jnp.concatenate([-slopes_d * dist[None, :, None, None], -slopes_a * pos[None, :, None, None]], axis=2)
        log_p = jnp.clip(score_scale[None, None, :, None] * s_raw, -20.0, 20.0)
        p_w = jnp.exp(log_p + log_w)
        phi = k_val[..., None] * theta
        re_k, im_k = (jnp.cos(phi), jnp.sin(phi))
        pw5 = p_w[..., None]
        merged = jnp.concatenate([p_w.squeeze(-1), (pw5 * re_k).reshape(b_, l, FLAT),
                                  (pw5 * im_k).reshape(b_, l, FLAT)], axis=-1)
        cs = jnp.cumsum(merged, axis=1)
        den = cs[..., :K_]
        num_re = cs[..., K_:K_ + FLAT]
        num_im = cs[..., K_ + FLAT:]
        inv_den = 1.0 / jnp.maximum(den[..., None], 0.0001)
        state_re = num_re.reshape(b_, l, K_, H_, M_) * inv_den[..., None]
        state_im = num_im.reshape(b_, l, K_, H_, M_) * inv_den[..., None]
        n_rep = K_ // KQ
        q_re = jnp.repeat(q_re, n_rep, axis=2)
        q_im = jnp.repeat(q_im, n_rep, axis=2)
        out_re = jnp.sum(state_re * q_re + state_im * q_im, axis=-1)
        out_im = jnp.sum(state_im * q_re - state_re * q_im, axis=-1)
        out_re = out_re * norm_scale
        out_im = out_im * norm_scale
        y = jnp.einsum('blkh,khn->blkn', out_re, W_re) + jnp.einsum('blkh,khn->blkn', out_im, W_im)
        y_val, y_gate = jnp.split(y, 2, axis=-1)
        y = (y_val * jax.nn.silu(y_gate)).reshape(b_, l, -1)
        return y @ W_out
    ao = fwd(_layernorm(x), jnp.asarray(ins["L0_attn_W_mem"]), jnp.asarray(ins["L0_attn_conv_k"]),
             jnp.asarray(ins["L0_attn_W_q"]), jnp.asarray(ins["L0_attn_theta"]),
             jnp.asarray(ins["L0_attn_decay_slopes"]), jnp.asarray(ins["L0_attn_anchor_slopes"]),
             jnp.asarray(ins["L0_attn_score_scale"]), jnp.asarray(ins["L0_attn_W_re"]),
             jnp.asarray(ins["L0_attn_W_im"]), jnp.asarray(ins["L0_attn_norm_scale"]),
             jnp.asarray(ins["L0_attn_W_out"]))
    return np.asarray(x + _layernorm(ao))[0]




def host_layouts_ffn(ins):
    w1g = np.asarray(ins["L0_ffn_w_gate"])
    w1u = np.asarray(ins["L0_ffn_w_up"])
    w2 = np.ascontiguousarray(np.asarray(ins["L0_ffn_w_down"]))
    w1g_t = np.ascontiguousarray(w1g.reshape(DC, 128, OT, 128).transpose(2, 1, 0, 3))
    w1u_t = np.ascontiguousarray(w1u.reshape(DC, 128, OT, 128).transpose(2, 1, 0, 3))
    sg_t = np.ascontiguousarray(np.asarray(ins["L0_ffn_s_gate"]).reshape(OT, 128).T)
    su_t = np.ascontiguousarray(np.asarray(ins["L0_ffn_s_up"]).reshape(OT, 128).T)
    sd_t = np.ascontiguousarray(np.asarray(ins["L0_ffn_s_down"]).reshape(1, D))
    return {"w1g": w1g_t, "w1u": w1u_t, "w2": w2, "sg": sg_t, "su": su_t, "sd": sd_t}


_NC_CACHE = [None]


def kernel(**inputs):
    ins = inputs
    lay = host_layouts(ins)
    lay.update(host_layouts_ffn(ins))
    if _NC_CACHE[0] is None:
        _NC_CACHE[0] = build()
    nc = _NC_CACHE[0]
    xs = np.asarray(ins["L0_attn_x"], np.float32)
    in_maps = []
    for b in range(8):
        feed = {"x": np.ascontiguousarray(xs[b])}
        feed.update(lay)
        in_maps.append(feed)
    res = run_bass_kernel_spmd(nc, in_maps, core_ids=list(range(8)))
    return np.stack([np.asarray(res.results[b]["out"], np.float32) for b in range(8)], axis=0)
